# version 51
# speedup vs baseline: 1.5223x; 1.0034x over previous
import sys

sys.path.insert(0, "/opt/trn_rl_repo")

import numpy as np
import ml_dtypes

import concourse.bass as bass
import concourse.mybir as mybir
from concourse.ap import AP
from concourse.bass_utils import run_bass_kernel_spmd

F32 = mybir.dt.float32
BF16 = mybir.dt.bfloat16
AF = mybir.ActivationFunctionType
ALU = mybir.AluOpType
AX = mybir.AxisListType

B, L, D = 8, 2048, 1024
H = 16
NE, DEXP = 8, 512
NCH = 8
NT = 16
EPS = 1e-5


class Sched:

    ENGINES = ("sync", "gpsimd", "scalar", "vector", "tensor")

    def __init__(self, nc):
        self.nc = nc
        self.events = {e: [] for e in self.ENGINES}
        self.sems = {}
        self.cnt = {}

    def sem(self, name):
        if name not in self.sems:
            self.sems[name] = self.nc.alloc_semaphore(name)
            self.cnt[name] = 0
        return name

    def ev(self, eng, fn, waits=(), inc=None, amt=1):
        ws = []
        for s, v in waits:
            assert s in self.cnt, f"wait on unknown sem {s}"
            assert v <= self.cnt[s], f"wait {s}>={v} but only {self.cnt[s]} scheduled"
            if v > 0:
                ws.append((s, v))
        self.events[eng].append((fn, tuple(ws), inc, amt))
        if inc is not None:
            self.sem(inc)
            self.cnt[inc] += amt
            return self.cnt[inc]
        return None

    def dma(self, eng, out, in_, waits=(), inc=None):
        return self.ev(eng, lambda e: e.dma_start(out=out, in_=in_),
                       waits=waits, inc=inc, amt=16)

    def emit(self, block):
        def mkbody(name):
            evs = self.events[name]
            sems = self.sems

            def body(eng):
                guaranteed = {}
                for fn, waits, inc, amt in evs:
                    for s, v in waits:
                        if guaranteed.get(s, 0) < v:
                            eng.wait_ge(sems[s], v)
                            guaranteed[s] = v
                    ins = fn(eng)
                    if inc is not None:
                        ins.then_inc(sems[inc], amt)
            return body

        block.sync(mkbody("sync"))
        block.gpsimd(mkbody("gpsimd"))
        block.scalar(mkbody("scalar"))
        block.vector(mkbody("vector"))
        block.tensor(mkbody("tensor"))


def build_program(phases=99, taps=()):
    nc = bass.Bass("TRN2", target_bir_lowering=False)
    sch = Sched(nc)
    for s in ("pe", "act", "dve", "misc",
              "x0", "x1", "wst0", "wst1", "wst2", "wst3",
              "ebA", "ebB", "ebC", "ebD", "cbw",
              "wg0", "wu0", "wd0", "wg1", "wu1", "wd1",
              "cb0", "cb1", "rtrw", "shw", "outq"):
        sch.sem(s)

    x_in = nc.dram_tensor("x", [L, D], F32, kind="ExternalInput")
    w_in = nc.dram_tensor("w_in", [3 * D, D], F32, kind="ExternalInput")
    k_off = nc.dram_tensor("k_off", [H, 16], F32, kind="ExternalInput")
    k_amp = nc.dram_tensor("k_amp", [H, 16], F32, kind="ExternalInput")
    k_sh = nc.dram_tensor("k_sh", [H, 16], F32, kind="ExternalInput")
    w_rtr = nc.dram_tensor("w_rtr", [D, NE], F32, kind="ExternalInput")
    w_eg = nc.dram_tensor("w_eg", [NE, D, DEXP], F32, kind="ExternalInput")
    w_eu = nc.dram_tensor("w_eu", [NE, D, DEXP], F32, kind="ExternalInput")
    w_ed = nc.dram_tensor("w_ed", [NE, DEXP, D], F32, kind="ExternalInput")
    w_sg = nc.dram_tensor("w_sg", [D, D], F32, kind="ExternalInput")
    w_su = nc.dram_tensor("w_su", [D, D], F32, kind="ExternalInput")
    w_sd = nc.dram_tensor("w_sd", [D, D], F32, kind="ExternalInput")
    y_out = nc.dram_tensor("y", [L, D], F32, kind="ExternalOutput")

    tdram = nc.dram_tensor("tdram", [16 * 512], F32, kind="Internal")
    expdram = nc.dram_tensor("expdram", [16 * 128 * 640 + 640], F32, kind="Internal")
    coefdram = nc.dram_tensor("coefdram", [NE * L], F32, kind="Internal")

    ident_bf = nc.inline_tensor(np.eye(128, dtype=ml_dtypes.bfloat16), "ident_bf")
    ident_f32 = nc.inline_tensor(np.eye(128, dtype=np.float32), "ident_f32")
    ones64 = nc.inline_tensor(np.ones((1, 64), dtype=ml_dtypes.bfloat16), "ones64c")
    hn_c = nc.inline_tensor(np.concatenate(
        [np.full((128, 1), 0.5, dtype=np.float32),
         np.full((128, 1), -1.0, dtype=np.float32),
         np.full((128, 1), 0.125, dtype=np.float32),
         np.full((128, 1), 1.0, dtype=np.float32)], axis=1), "hn_c")
    drow = nc.inline_tensor((np.arange(512, dtype=np.float32) - 255.0)[None, :], "drow")
    sel = np.zeros((2, 128, 16), dtype=np.float32)
    for hb in range(2):
        for p in range(128):
            sel[hb, p, hb * 8 + p // 16] = 1.0
    sel_c = nc.inline_tensor(np.ascontiguousarray(sel.reshape(2 * 128, 16)), "sel_c")

    resid = nc.alloc_sbuf_tensor("resid", [128, NCH * L], F32)
    xn = nc.alloc_sbuf_tensor("xn", [128, NCH * L], BF16)
    id_bf = nc.alloc_sbuf_tensor("id_bf", [128, 128], BF16)
    id_f32 = nc.alloc_sbuf_tensor("id_f32", [128, 128], F32)
    ones_sb = nc.alloc_sbuf_tensor("ones_sb", [128, 64], BF16)
    sel_sb = nc.alloc_sbuf_tensor("sel_sb", [128, 32], F32)
    rtr_sb = nc.alloc_sbuf_tensor("rtr_sb", [128, NCH * NE], BF16)
    hn_sb = nc.alloc_sbuf_tensor("hn_sb", [128, 4], F32)
    AR_W = nc.alloc_sbuf_tensor("AR_W", [128, 12288], F32)
    AR_M = nc.alloc_sbuf_tensor("AR_M", [128, 7952], F32)
    AR_E = nc.alloc_sbuf_tensor("AR_E", [128, 3072], F32)
    w_pair = nc.alloc_sbuf_tensor("w_pair", [128, 3072], BF16)
    wstage = nc.alloc_sbuf_tensor("wstage", [128, 2048], BF16)
    h2t4 = nc.alloc_sbuf_tensor("h2t4", [128, 2048], F32)

    pB = [nc.alloc_psum_tensor(f"pb{i}", [128, 512], F32) for i in range(8)]
    pBb = [pB[i][:, 0:512].bitcast(BF16) for i in range(8)]

    regw = {f"B{i}": [] for i in range(8)}

    def pe_w(*regions):
        ws = []
        for r in regions:
            ws += regw[r]
        return ws

    def consumed(region, sem):
        regw[region] = [(sem, sch.cnt[sem])]

    tapdefs = []
    ST = {"lrd": 0}

    def tap(name, ap, waits):
        if name in taps:
            th = nc.dram_tensor("tap_" + name, list(ap.shape), ap.dtype,
                                kind="ExternalOutput")
            sch.dma("sync", th[:], ap, waits=waits, inc="misc")

    sch.dma("sync", id_bf[:], ident_bf[:], inc="misc")
    sch.dma("sync", id_f32[:], ident_f32[:], inc="misc")
    sch.dma("sync", ones_sb[:], AP(ones64, 0, [[0, 128], [1, 64]]), inc="misc")
    sch.dma("sync", sel_sb[:, 0:16], AP(sel_c, 0, [[16, 128], [1, 16]]), inc="misc")
    sch.dma("sync", sel_sb[:, 16:32], AP(sel_c, 128 * 16, [[16, 128], [1, 16]]), inc="misc")
    sch.dma("sync", hn_sb[:], hn_c[:], inc="misc")
    consts = ("misc", sch.cnt["misc"])

    tb_cols = AR_M[:, 6160:6176]
    tb_selamp = AR_M[:, 6176:6192]
    tb_dbc = AR_M[:, 6192:6704]
    tb_tmp = h2t4[:, 0:512]
    tb_ebt = AR_M[:, 7216:7728]
    eb_all = AR_E[:, 0:3072].bitcast(BF16)

    sch.dma("scalar", tb_dbc[:], AP(drow, 0, [[0, 128], [1, 512]]), inc="ebA")
    for hb in range(2):
        base = hb * 128
        sch.dma("scalar", tb_cols[:, 0 + hb:1 + hb], AP(k_off, base, [[1, 128], [1, 1]]), inc="ebA")
        sch.dma("scalar", tb_cols[:, 2 + hb:3 + hb], AP(k_amp, base, [[1, 128], [1, 1]]), inc="ebA")
        sch.dma("scalar", tb_cols[:, 4 + hb:5 + hb], AP(k_sh, base, [[1, 128], [1, 1]]), inc="ebA")
    ebl = ("ebA", sch.cnt["ebA"])

    tb_tmp2 = h2t4[:, 512:1024]

    def emit_p1_chain():
      prev_half = []
      for hb in range(2):
        oc = tb_cols[:, 0 + hb:1 + hb]
        ac = tb_cols[:, 2 + hb:3 + hb]
        sc = tb_cols[:, 4 + hb:5 + hb]
        d_sub = sch.ev("vector", lambda e, oc=oc: e.tensor_scalar(
            tb_tmp[:], tb_dbc[:], oc, None, ALU.subtract),
            waits=[ebl] + prev_half, inc="dve")
        d_sq = sch.ev("vector", lambda e: e.tensor_mul(tb_tmp[:], tb_tmp[:], tb_tmp[:]),
                      waits=[("dve", d_sub)], inc="dve")
        d_t = sch.ev("vector", lambda e, sc=sc: e.tensor_scalar(
            tb_tmp[:], tb_tmp[:], sc, None, ALU.mult), waits=[("dve", d_sq)], inc="dve")
        d_n = sch.ev("vector", lambda e: e.tensor_scalar(
            tb_tmp2[:], tb_tmp[:], -1.0, None, ALU.mult), waits=[("dve", d_t)], inc="dve")
        d3 = sch.ev("vector", lambda e: e.tensor_tensor(
            tb_tmp[:], tb_tmp[:], tb_tmp2[:], ALU.min), waits=[("dve", d_n)], inc="dve")
        sch.ev("scalar", lambda e: e.activation(tb_tmp[:], tb_tmp[:], AF.Exp),
               waits=[("dve", d3)], inc="act")
        a3 = sch.cnt["act"]
        sch.ev("vector", lambda e, hb=hb, ac=ac: e.tensor_scalar(
            tb_selamp[:], sel_sb[:, hb * 16:(hb + 1) * 16], ac, None, ALU.mult),
            waits=[consts], inc="dve")
        d4 = sch.cnt["dve"]
        sch.ev("tensor", lambda e, hb=hb: e.matmul(pB[1][0:16, :], tb_selamp[:], tb_tmp[:],
                                                   start=(hb == 0), stop=(hb == 1)),
               waits=pe_w("B1") + [("act", a3), ("dve", d4)], inc="pe")
        prev_half = [("act", a3), ("pe", sch.cnt["pe"])]
      p_T = sch.cnt["pe"]
      tap("tb_tmp", tb_tmp[:], [("act", sch.cnt["act"])])
      tap("tb_selamp", tb_selamp[:], [("dve", sch.cnt["dve"])])
      tap("tb_cols", tb_cols[:], [("dve", sch.cnt["dve"])])
      sch.ev("scalar", lambda e: e.activation(tb_ebt[0:16, :], pB[1][0:16, :], AF.Exp),
           waits=[("pe", p_T)], inc="act")
      tap("tb_ebt", tb_ebt[:], [("act", sch.cnt["act"])])
      consumed("B1", "act")
      a_ebt = sch.cnt["act"]
      sch.dma("scalar", AP(tdram, 0, [[512, 16], [1, 512]]), tb_ebt[0:16, :],
            waits=[("act", a_ebt)], inc="ebB")


    if phases <= 1:
        return _finish(nc, sch)

    coef_all = AR_M[:, 5632:5760]
    coef_scr = AR_M[:, 5760:5792]
    e8_all = AR_M[:, 5120:5248]
    coefT_sb = h2t4[0:8, 0:2048]
    coef_wr = None
    rtab = {"d_c": [None] * NT, "done": 0, "p_all": 0}

    sch.dma("gpsimd", rtr_sb.rearrange("p (c e) -> p c e", c=NCH),
            AP(w_rtr, 0, [[NE, 128], [128 * NE, NCH], [1, NE]]),
            inc="rtrw")
    rtr_rd = ("rtrw", sch.cnt["rtrw"])

    def router_exp():
        sch.ev("scalar", lambda e: e.activation(e8_all[:], pB[0][:, 0:128], AF.Exp),
               waits=[("pe", rtab["p_all"])], inc="act")
        consumed("B0", "act")
        rtab["a_e"] = sch.cnt["act"]

    def router_chain_step(tc):
        st4 = coef_scr[:, 8:12]
        probs = coef_scr[:, 12:20]
        mask = coef_scr[:, 20:28]
        if True:
            e8 = e8_all[:, tc * 8:(tc + 1) * 8]
            dd = sch.ev("vector", lambda e, e8=e8: e.reduce_sum(
                st4[:, 0:1], e8, axis=AX.X),
                waits=([("act", rtab["a_e"])] if tc == 0 else []), inc="dve")
            dd = sch.ev("vector", lambda e: e.reciprocal(st4[:, 1:2], st4[:, 0:1]),
                        waits=[("dve", dd)], inc="dve")
            dd = sch.ev("vector", lambda e, e8=e8: e.tensor_scalar(
                probs, e8, st4[:, 1:2], None, ALU.mult),
                waits=[("dve", dd)], inc="dve")
            dd = sch.ev("vector", lambda e: e.reduce_max(st4[:, 2:3], probs, axis=AX.X),
                        waits=[("dve", dd)], inc="dve")
            dd = sch.ev("vector", lambda e: e.tensor_scalar(
                mask, probs, st4[:, 2:3], -1e30, ALU.is_equal, ALU.mult),
                waits=[("dve", dd)], inc="dve")
            dd = sch.ev("vector", lambda e: e.tensor_add(mask, mask, probs),
                        waits=[("dve", dd)], inc="dve")
            dd = sch.ev("vector", lambda e: e.reduce_max(st4[:, 3:4], mask, axis=AX.X),
                        waits=[("dve", dd)], inc="dve")
            dd = sch.ev("vector", lambda e: e.tensor_scalar(
                mask, probs, st4[:, 3:4], None, ALU.is_ge),
                waits=[("dve", dd)], inc="dve")
            dd = sch.ev("vector", lambda e, tc=tc: e.tensor_mul(
                coef_all[:, tc * 8:(tc + 1) * 8], mask, probs),
                waits=[("dve", dd)], inc="dve")
            rtab["d_c"][tc] = sch.cnt["dve"]
            if tc == NT - 1:
                av_ep_cons[0] = [("dve", sch.cnt["dve"])]

    def coefT_step():
        tc = rtab["done"]
        if tc >= NT:
            return
        rtab["done"] += 1
        sch.ev("tensor", lambda e, tc=tc: e.transpose(
            pB[7][0:8, 0:128], coef_all[:, tc * 8:(tc + 1) * 8], id_f32[:]),
            waits=pe_w("B7") + [("dve", rtab["d_c"][tc])], inc="pe")
        p_t = sch.cnt["pe"]
        sch.ev("vector", lambda e, tc=tc: e.tensor_copy(
            coefT_sb[:, tc * 128:(tc + 1) * 128], pB[7][0:8, 0:128]),
            waits=[("pe", p_t)] + ([("ebD", 64)] if tc == 0 else []), inc="dve")
        consumed("B7", "dve")
        if tc == NT - 1:
            nonlocal coef_wr
            d_ct = sch.cnt["dve"]
            sch.dma("scalar", AP(coefdram, 0, [[2048, 8], [1, 2048]]), coefT_sb[:],
                    waits=[("dve", d_ct)], inc="cbw")
            coef_wr = ("cbw", sch.cnt["cbw"])
            av_ep_cons[1] = [("pe", p_t)]
            av_ep_cons[2] = [("pe", p_t)]


    stage = [AR_M[:, 0:2048], AR_M[:, 2048:4096]]
    xn_tm = [[AR_M[:, 4096:4608].bitcast(BF16), AR_M[:, 4608:5120].bitcast(BF16)],
             [AR_M[:, 5120:5632].bitcast(BF16), AR_M[:, 5632:6144].bitcast(BF16)]]
    sq_scr_p = [AR_M[:, 4096:5120], AR_M[:, 5120:6144]]
    lncol = AR_M[:, 6144:6160]

    xsem = ["x0", "x1"]
    xfree = [[], []]
    ln_prog = []
    xv_p = {}
    a_xn_p = {}

    def ln_stats(g):
        sl = g % 2
        xv = sch.dma("sync" if sl == 0 else "scalar",
                     stage[sl].rearrange("p (t c) -> p t c", t=2),
                     AP(x_in, g * 256 * D, [[D, 128], [128 * D, 2], [1, D]]),
                     waits=xfree[sl], inc=xsem[sl])
        xv_p[g] = xv
        a0l = 0
        for ti in range(2):
            xt = stage[sl][:, ti * 1024:(ti + 1) * 1024]
            sch.ev("vector", lambda e, xt=xt, ti=ti: e.reduce_sum(
                lncol[:, ti:ti + 1], xt, axis=AX.X), waits=[(xsem[sl], xv)], inc="dve")
            a0l = sch.ev("scalar", lambda e, xt=xt, ti=ti, sl=sl: e.activation(
                sq_scr_p[sl][:], xt, AF.Square, accum_out=lncol[:, 2 + ti:3 + ti]),
                waits=[(xsem[sl], xv)], inc="act")
        d_s = sch.cnt["dve"]
        d_nm = sch.ev("vector", lambda e: e.tensor_scalar(
            lncol[:, 4:6], lncol[:, 0:2], -1.0 / D, None, ALU.mult),
            waits=[("dve", d_s)], inc="dve")
        d_m2 = sch.ev("vector", lambda e: e.tensor_scalar(
            lncol[:, 6:8], lncol[:, 2:4], 1.0 / D, None, ALU.mult),
            waits=[("act", a0l)], inc="dve")
        d_nq = sch.ev("vector", lambda e: e.scalar_tensor_tensor(
            lncol[:, 8:10], lncol[:, 4:6], -1.0, lncol[:, 4:6],
            ALU.mult, ALU.mult), waits=[("dve", d_nm)], inc="dve")
        d1 = sch.ev("vector", lambda e: e.scalar_tensor_tensor(
            lncol[:, 8:10], lncol[:, 8:10], EPS, lncol[:, 6:8],
            ALU.add, ALU.add), waits=[("dve", d_nq), ("dve", d_m2)], inc="dve")
        a1h = sch.ev("scalar", lambda e: e.activation(
            lncol[:, 10:12], lncol[:, 8:10], AF.Ln),
            waits=[("dve", d1)], inc="act")
        d1h = sch.ev("vector", lambda e: e.tensor_scalar(
            lncol[:, 10:12], lncol[:, 10:12], -0.5, None, ALU.mult),
            waits=[("act", a1h)], inc="dve")
        a1 = sch.ev("scalar", lambda e: e.activation(
            lncol[:, 12:14], lncol[:, 10:12], AF.Exp),
            waits=[("dve", d1h)], inc="act")
        a_xn = []
        for ti in range(2):
            xt = stage[sl][:, ti * 1024:(ti + 1) * 1024]
            sch.ev("vector", lambda e, xt=xt, sl=sl, ti=ti: e.tensor_scalar(
                xn_tm[sl][ti][:], xt, lncol[:, 4 + ti:5 + ti],
                lncol[:, 12 + ti:13 + ti], ALU.add, ALU.mult),
                waits=[("act", a1), ("dve", d_nm)], inc="dve")
            a_xn.append(("dve", sch.cnt["dve"]))
        a_xn_p[g] = a_xn

    def ln_finish(g):
        sl = g % 2
        xv = xv_p[g]
        a_xn = a_xn_p[g]
        for dc in range(NCH):
            rb = ["B6", "B7"][dc % 2]
            rf = ["B4", "B5"][dc % 2]
            pb = pBb[6 + dc % 2][:, 0:256]
            pf = pB[4 + dc % 2][:, 0:256]
            for ti in range(2):
                sch.ev("tensor", lambda e, ti=ti, dc=dc, pb=pb, sl=sl: e.transpose(
                    pb[:, ti * 128:(ti + 1) * 128],
                    xn_tm[sl][ti][:, dc * 128:(dc + 1) * 128], id_bf[:]),
                    waits=pe_w(rb) + [a_xn[ti], consts], inc="pe")
                sch.ev("tensor", lambda e, ti=ti, dc=dc, pf=pf, sl=sl: e.transpose(
                    pf[:, ti * 128:(ti + 1) * 128],
                    stage[sl][:, ti * 1024 + dc * 128: ti * 1024 + (dc + 1) * 128],
                    id_f32[:]),
                    waits=pe_w(rf) + [(xsem[sl], xv)], inc="pe")
            p_tr = sch.cnt["pe"]
            sch.ev("vector", lambda e, dc=dc, g=g, pb=pb: e.tensor_copy(
                xn[:, dc * L + g * 256: dc * L + g * 256 + 256], pb[:]),
                waits=[("pe", p_tr)], inc="dve")
            consumed(rb, "dve")
            sch.ev("scalar", lambda e, dc=dc, g=g, pf=pf: e.copy(
                resid[:, dc * L + g * 256: dc * L + g * 256 + 256], pf[:]),
                waits=[("pe", p_tr)], inc="act")
            consumed(rf, "act")
        xfree[sl] = [("pe", sch.cnt["pe"]), a_xn[1]]
        ln_prog.append((sch.cnt["dve"], sch.cnt["act"]))
        for tc2 in (2 * g, 2 * g + 1):
            w0r = (pe_w("B0") + [rtr_rd] if tc2 == 0 else []) + \
                  [("dve", ln_prog[g][0])]
            for dc in range(NCH):
                sch.ev("tensor", lambda e, tc2=tc2, dc=dc: e.matmul(
                    pB[0][:, tc2 * 8:(tc2 + 1) * 8],
                    xn[:, dc * L + tc2 * 128:dc * L + (tc2 + 1) * 128],
                    rtr_sb[:, dc * 8:(dc + 1) * 8],
                    start=(dc == 0), stop=(dc == NCH - 1)),
                    waits=(w0r if dc == 0 else []), inc="pe")
        if g == 7:
            rtab["p_all"] = sch.cnt["pe"]

    Wg_sh = AR_W[:, 0:4096].bitcast(BF16)
    Wu_sh = AR_W[:, 4096:8192].bitcast(BF16)
    Wd_sh = AR_W[:, 8192:12288].bitcast(BF16)
    h_s = AR_E[:, 0:2048].bitcast(BF16)
    gs = AR_M[:, 6928:7184].bitcast(BF16)

    wpat = [[D, 128], [128 * D, NCH], [1, D]]
    sch.dma("gpsimd", Wg_sh.rearrange("p (c n) -> p c n", c=NCH), AP(w_sg, 0, wpat),
            inc="shw")
    sch.dma("gpsimd", Wu_sh.rearrange("p (c n) -> p c n", c=NCH), AP(w_su, 0, wpat),
            inc="shw")
    sch.dma("gpsimd", Wd_sh.rearrange("p (c n) -> p c n", c=NCH), AP(w_sd, 0, wpat),
            inc="shw")
    sh_rd = ("shw", sch.cnt["shw"])


    wsem = ["wst0", "wst1", "wst2"]
    wst_v = [0, 0, 0]
    wst_slot = {0: 0, 1: 1, 2: 0}
    ROWB = {0: 2048, 1: 0, 2: 1024}

    def wpair_load(j, which, waits):
        sl = wst_slot[which]
        wst_v[which] = sch.dma(
            "gpsimd", wstage[:, sl * 1024:(sl + 1) * 1024],
            AP(w_in, (ROWB[which] + j * 128) * D, [[D, 128], [1, D]]),
            waits=waits, inc=wsem[which])

    def wpair_prep(j, extra, banks):
        for which in range(3):
            sl = wst_slot[which]
            if which == 2:
                wpair_load(j, 2, [("pe", sch.cnt["pe"])])
            for g2 in range(2):
                bi = banks[(2 * which + g2) % len(banks)]
                reg = f"B{bi}"
                pb = pBb[bi][:, 0:512]
                for i in range(4):
                    dc = g2 * 4 + i
                    sch.ev("tensor", lambda e, which=which, dc=dc, i=i, pb=pb, sl=sl: e.transpose(
                        pb[:, i * 128:(i + 1) * 128],
                        wstage[:, sl * 1024 + dc * 128: sl * 1024 + (dc + 1) * 128],
                        id_bf[:]),
                        waits=(pe_w(reg) + [(wsem[which], wst_v[which]), consts] + extra
                               if i == 0 else []), inc="pe")
                p_tr = sch.cnt["pe"]
                sch.ev("scalar", lambda e, which=which, g2=g2, pb=pb: e.copy(
                    w_pair[:, (which * 8 + g2 * 4) * 128: (which * 8 + g2 * 4 + 4) * 128],
                    pb[:]),
                    waits=[("pe", p_tr)], inc="act")
                consumed(reg, "act")

    if phases <= 2:
        return _finish(nc, sch)

    Qp = AR_M[:, 0:1024].bitcast(BF16)
    Kp = AR_M[:, 1024:2048].bitcast(BF16)
    Vaug = AR_M[:, 2048:3088].bitcast(BF16)
    Vfm = AR_M[:, 3088:4112].bitcast(BF16)
    attb = [AR_M[:, 4112 + i * 256: 4368 + i * 256].bitcast(BF16) for i in range(4)]
    rb_sb = AR_M[0:64, 6672:6928].bitcast(BF16)
    tmp_ep = AR_M[:, 7184:7440].bitcast(BF16)
    rrow = [AR_M[0:1, 7440:7696].bitcast(BF16),
            AR_M[0:1, 7696:7952].bitcast(BF16)]
    av_ep = [AR_M[0:65, 5136 + i * 256: 5392 + i * 256].bitcast(BF16)
             for i in range(4)]

    last_scores_pe = 0
    last_vaug_pe = 0
    proj_evac_dve = 0
    attb_cons = [[] for _ in range(4)]
    avdone = {"B4": list(regw["B4"]), "B5": list(regw["B5"])}
    av_ep_cons = [[], [], [], []]
    rrow_cons = [[], []]
    rb_sb_cons = []
    pend = []

    def epi_stage1():
        for P in pend:
            sch.ev("scalar", lambda e, P=P: e.activation(
                rrow[P["hh"]][:], av_ep[P["slot"]][64:65, :], AF.Ln),
                waits=[("act", P["evac"]), consts] + rrow_cons[P["hh"]], inc="act")
            P["d_r"] = sch.ev("scalar", lambda e, P=P: e.activation(
                rrow[P["hh"]][:], rrow[P["hh"]][:], AF.Exp,
                scale=hn_sb[0:1, 1:2]),
                waits=[("act", sch.cnt["act"])], inc="act")

    def epi_stage2():
        for P in pend:
            hh, q4p, h = P["hh"], P["q4"], P["h"]
            sch.ev("tensor", lambda e, P=P: e.matmul(
                pB[7][0:64, :], ones_sb[0:1, :], rrow[P["hh"]][:],
                start=True, stop=True),
                waits=pe_w("B7") + [("act", P["d_r"])], inc="pe")
            rrow_cons[hh] = [("pe", sch.cnt["pe"])]
            p_rb = sch.cnt["pe"]
            sch.ev("scalar", lambda e: e.copy(rb_sb[:], pB[7][0:64, :]),
                   waits=[("pe", p_rb)] + rb_sb_cons, inc="act")
            consumed("B7", "act")
            a_rb = sch.cnt["act"]
            rbase = (h % 2) * 64
            d_m = sch.ev("vector", lambda e, P=P, rbase=rbase: e.tensor_mul(
                tmp_ep[rbase:rbase + 64, :], av_ep[P["slot"]][0:64, :], rb_sb[:]),
                waits=[("act", a_rb)], inc="dve")
            rb_sb_cons[:] = [("dve", d_m)]
            roff = (h // 2) * L + q4p * 512
            ST["lrd"] = sch.ev(
                "vector", lambda e, rbase=rbase, roff=roff: e.tensor_add(
                    resid[rbase:rbase + 64, roff:roff + 512],
                    resid[rbase:rbase + 64, roff:roff + 512],
                    tmp_ep[rbase:rbase + 64, :]),
                waits=[("dve", d_m), ("dve", ST["lrd"])], inc="dve")
            av_ep_cons[P["slot"]] = [("dve", sch.cnt["dve"])]


    cbs = [AR_M[:, 6160:6416].bitcast(BF16),
           AR_M[:, 6416:6672].bitcast(BF16)]
    gs = AR_M[:, 6928:7184].bitcast(BF16)
    h2v = h2t4[:, 0:2048].bitcast(BF16)

    exp_area = {}
    WV_G0 = [0, 0]
    WV_U0 = [0, 0]
    WV_D0 = [0, 0]
    cb_v = [0, 0]
    cb_cons = [[], []]
    cb_seq = [(g, t, s2) for g in range(4) for t in range(4) for s2 in range(2)]
    egpat0 = [[DEXP, 128], [128 * DEXP, NCH], [1, DEXP]]
    edpat0 = [[D, 128], [128 * D, 4], [1, D]]

    def emit_cb(k):
        if k >= len(cb_seq):
            return
        g, t, s2 = cb_seq[k]
        ex = 2 * g + s2
        sl = k % 2
        assert coef_wr is not None
        cb_v[sl] = sch.dma("gpsimd", cbs[sl][:],
                           AP(coefdram, ex * L + t * 512, [[0, 128], [1, 512]]),
                           waits=[coef_wr] + cb_cons[sl], inc=f"cb{sl}")

    def shared_work():
        gs_cons2 = []
        hs_cons = []
        for t4 in range(4):
            for dsc in range(NCH):
                w0 = pe_w("B6") + [sh_rd, ("dve", ln_prog[2 * t4 + 1][0])]
                for hc in range(NCH):
                    sch.ev("tensor", lambda e, t4=t4, dsc=dsc, hc=hc: e.matmul(
                        pB[6][:], Wg_sh[:, hc * 1024 + dsc * 128: hc * 1024 + (dsc + 1) * 128],
                        xn[:, hc * L + t4 * 512: hc * L + (t4 + 1) * 512],
                        start=(hc == 0), stop=(hc == NCH - 1)),
                        waits=(w0 if hc == 0 else []), inc="pe")
                p_g = sch.cnt["pe"]
                w1 = pe_w("B7")
                for hc in range(NCH):
                    sch.ev("tensor", lambda e, t4=t4, dsc=dsc, hc=hc: e.matmul(
                        pB[7][:], Wu_sh[:, hc * 1024 + dsc * 128: hc * 1024 + (dsc + 1) * 128],
                        xn[:, hc * L + t4 * 512: hc * L + (t4 + 1) * 512],
                        start=(hc == 0), stop=(hc == NCH - 1)),
                        waits=(w1 if hc == 0 else []), inc="pe")
                p_u = sch.cnt["pe"]
                sch.ev("scalar", lambda e: e.activation(gs[:], pB[6][:], AF.Tanh,
                                                        scale=hn_sb[:, 0:1]),
                       waits=[("pe", p_g), consts] + gs_cons2, inc="act")
                hsv = h_s[:, dsc * 512:(dsc + 1) * 512]
                sch.ev("scalar", lambda e, hsv=hsv: e.activation(
                    hsv, pB[6][:], AF.Copy, scale=hn_sb[:, 0:1]),
                    waits=hs_cons, inc="act")
                hs_cons = []
                consumed("B6", "act")
                a_g = sch.cnt["act"]
                dd = sch.ev("vector", lambda e, hsv=hsv: e.scalar_tensor_tensor(
                    hsv, gs[:], 1.0, hsv, ALU.add, ALU.mult),
                    waits=[("act", a_g)], inc="dve")
                gs_cons2 = [("dve", sch.cnt["dve"])]
                dd = sch.ev("vector", lambda e, hsv=hsv: e.tensor_mul(hsv, hsv, pB[7][:]),
                            waits=[("pe", p_u), ("dve", dd)], inc="dve")
                consumed("B7", "dve")
                yield 16
            if t4 == 3:
                p_up = sch.cnt["pe"]
                WV_G0[0] = sch.dma(
                    "gpsimd", AR_W[:, 0:2048].bitcast(BF16).rearrange(
                        "p (c n) -> p c n", c=NCH),
                    AP(w_eg, 0, egpat0), waits=[("pe", p_up)], inc="wg0")
                WV_U0[0] = sch.dma(
                    "gpsimd", AR_W[:, 2048:4096].bitcast(BF16).rearrange(
                        "p (c n) -> p c n", c=NCH),
                    AP(w_eu, 0, egpat0), waits=[("pe", p_up)], inc="wu0")
                WV_D0[0] = sch.dma(
                    "gpsimd", AR_W[:, 4096:6144].bitcast(BF16).rearrange(
                        "p (c n) -> p c n", c=4),
                    AP(w_ed, 0, edpat0), waits=[("pe", p_up)], inc="wd0")
                WV_G0[1] = sch.dma(
                    "gpsimd", AR_W[:, 6144:8192].bitcast(BF16).rearrange(
                        "p (c n) -> p c n", c=NCH),
                    AP(w_eg, D * DEXP, egpat0), waits=[("pe", p_up)], inc="wg1")
            d_h = sch.cnt["dve"]
            for hc in range(NCH):
                pdi = 6 + hc % 2
                reg = f"B{pdi}"
                w0 = pe_w(reg) + [("dve", d_h)]
                for dsc in range(NCH):
                    sch.ev("tensor", lambda e, t4=t4, hc=hc, dsc=dsc, pdi=pdi: e.matmul(
                        pB[pdi][:], Wd_sh[:, dsc * 1024 + hc * 128: dsc * 1024 + (hc + 1) * 128],
                        h_s[:, dsc * 512:(dsc + 1) * 512],
                        start=(dsc == 0), stop=(dsc == NCH - 1)),
                        waits=(w0 if dsc == 0 else []), inc="pe")
                p_d = sch.cnt["pe"]
                roff = hc * L + t4 * 512
                ST["lrd"] = sch.ev("vector", lambda e, roff=roff, pdi=pdi: e.tensor_add(
                    resid[:, roff:roff + 512], resid[:, roff:roff + 512], pB[pdi][:]),
                    waits=[("pe", p_d), ("dve", ST["lrd"]),
                           ("act", ln_prog[2 * t4 + 1][1])], inc="dve")
                consumed(reg, "dve")
                yield 8
            hs_cons = [("pe", sch.cnt["pe"])]
        sh_done = sch.cnt["pe"]
        emit_cb(0)
        emit_cb(1)
        WV_U0[1] = sch.dma(
            "gpsimd", AR_W[:, 8192:10240].bitcast(BF16).rearrange(
                "p (c n) -> p c n", c=NCH),
            AP(w_eu, D * DEXP, egpat0), waits=[("pe", sh_done)], inc="wu1")
        WV_D0[1] = sch.dma(
            "gpsimd", AR_W[:, 10240:12288].bitcast(BF16).rearrange(
                "p (c n) -> p c n", c=4),
            AP(w_ed, D * DEXP, edpat0), waits=[("pe", sh_done)], inc="wd1")
        sch.dma("gpsimd", eb_all[:, 0:11 * 384].rearrange("p (h j) -> p h j", h=11),
                AP(expdram, 127, [[640, 128], [81920, 11], [1, 384]]),
                waits=[("ebD", 64), ("pe", sh_done)], inc="ebA")
        ST["gs_cons"] = list(gs_cons2)

    def expert_work():
        area = exp_area
        h2cons = [("ebD", 64)]
        gs_cons2 = list(ST.get("gs_cons", []))
        wv_g = list(WV_G0)
        wv_u = list(WV_U0)
        wv_d = list(WV_D0)
        wsem_g = ["wg0", "wg1"]
        wsem_u = ["wu0", "wu1"]
        wsem_d = ["wd0", "wd1"]
        egpat = [[DEXP, 128], [128 * DEXP, NCH], [1, DEXP]]
        edpat = [[D, 128], [128 * D, 4], [1, D]]
        for grp in range(4):
            if grp == 0:
                continue_loads = False
            else:
                continue_loads = True
            for si in range(2):
                if not continue_loads:
                    break
                ex = 2 * grp + si
                gv = AR_W[:, si * 6144:si * 6144 + 2048].bitcast(BF16)
                uv = AR_W[:, si * 6144 + 2048:si * 6144 + 4096].bitcast(BF16)
                dv = AR_W[:, si * 6144 + 4096:si * 6144 + 6144].bitcast(BF16)
                wq = [("pe", area[f"g{si}"])]
                wv_g[si] = sch.dma("gpsimd", gv.rearrange("p (c n) -> p c n", c=NCH),
                                   AP(w_eg, ex * D * DEXP, egpat), waits=wq, inc=wsem_g[si])
                wq = [("pe", area[f"u{si}"])]
                wv_u[si] = sch.dma("gpsimd", uv.rearrange("p (c n) -> p c n", c=NCH),
                                   AP(w_eu, ex * D * DEXP, egpat), waits=wq, inc=wsem_u[si])
                wq = [("pe", area["d"])]
                wv_d[si] = sch.dma("gpsimd", dv.rearrange("p (c n) -> p c n", c=4),
                                   AP(w_ed, ex * D * DEXP, edpat), waits=wq, inc=wsem_d[si])
            for t4 in range(4):
                for si in range(2):
                    kblk = (grp * 4 + t4) * 2 + si
                    scb = kblk % 2
                    cbv = cbs[scb]
                    gv = AR_W[:, si * 6144:si * 6144 + 2048].bitcast(BF16)
                    uv = AR_W[:, si * 6144 + 2048:si * 6144 + 4096].bitcast(BF16)
                    for dk in range(4):
                        w0 = pe_w("B6") + [(wsem_g[si], wv_g[si]), ("dve", ln_dve)]
                        for hc in range(NCH):
                            sch.ev("tensor", lambda e, t4=t4, dk=dk, hc=hc, gv=gv: e.matmul(
                                pB[6][:], gv[:, hc * 512 + dk * 128: hc * 512 + (dk + 1) * 128],
                                xn[:, hc * L + t4 * 512: hc * L + (t4 + 1) * 512],
                                start=(hc == 0), stop=(hc == NCH - 1)),
                                waits=(w0 if hc == 0 else []), inc="pe")
                        p_g = sch.cnt["pe"]
                        area[f"g{si}"] = p_g
                        w1 = pe_w("B7") + [(wsem_u[si], wv_u[si])]
                        for hc in range(NCH):
                            sch.ev("tensor", lambda e, t4=t4, dk=dk, hc=hc, uv=uv: e.matmul(
                                pB[7][:], uv[:, hc * 512 + dk * 128: hc * 512 + (dk + 1) * 128],
                                xn[:, hc * L + t4 * 512: hc * L + (t4 + 1) * 512],
                                start=(hc == 0), stop=(hc == NCH - 1)),
                                waits=(w1 if hc == 0 else []), inc="pe")
                        p_u = sch.cnt["pe"]
                        area[f"u{si}"] = p_u
                        sch.ev("scalar", lambda e: e.activation(gs[:], pB[6][:], AF.Tanh,
                                                                scale=hn_sb[:, 0:1]),
                               waits=[("pe", p_g), consts] + gs_cons2, inc="act")
                        hv = h2v[:, si * 2048 + dk * 512: si * 2048 + (dk + 1) * 512]
                        sch.ev("scalar", lambda e, hv=hv: e.activation(
                            hv, pB[6][:], AF.Copy, scale=hn_sb[:, 0:1]),
                            waits=h2cons, inc="act")
                        h2cons = []
                        consumed("B6", "act")
                        a_g = sch.cnt["act"]
                        dd = sch.ev("vector", lambda e, hv=hv: e.scalar_tensor_tensor(
                            hv, gs[:], 1.0, hv, ALU.add, ALU.mult),
                            waits=[("act", a_g)], inc="dve")
                        gs_cons2 = [("dve", sch.cnt["dve"])]
                        dd = sch.ev("vector", lambda e, hv=hv: e.tensor_mul(hv, hv, pB[7][:]),
                                    waits=[("pe", p_u), ("dve", dd)], inc="dve")
                        consumed("B7", "dve")
                        sch.ev("vector", lambda e, hv=hv, cbv=cbv: e.tensor_mul(
                            hv, hv, cbv[:]),
                            waits=[(f"cb{scb}", cb_v[scb]), ("dve", dd)], inc="dve")
                        yield 16
                    cb_cons[scb] = [("dve", sch.cnt["dve"])]
                    emit_cb(kblk + 2)
                d_h = sch.cnt["dve"]
                for hc in range(NCH):
                    pdi = 6 + hc % 2
                    reg = f"B{pdi}"
                    w0 = pe_w(reg) + [("dve", d_h),
                                      (wsem_d[0], wv_d[0]), (wsem_d[1], wv_d[1])]
                    first = True
                    for si in range(2):
                        dvv = AR_W[:, si * 6144 + 4096:si * 6144 + 6144].bitcast(BF16)
                        for dk in range(4):
                            sch.ev("tensor",
                                   lambda e, t4=t4, hc=hc, dk=dk, si=si, pdi=pdi, dvv=dvv:
                                   e.matmul(pB[pdi][:],
                                            dvv[:, dk * 1024 + hc * 128: dk * 1024 + (hc + 1) * 128],
                                            h2v[:, si * 2048 + dk * 512: si * 2048 + (dk + 1) * 512],
                                            start=(si == 0 and dk == 0),
                                            stop=(si == 1 and dk == 3)),
                                   waits=(w0 if first else []), inc="pe")
                            first = False
                    p_d = sch.cnt["pe"]
                    area["d"] = p_d
                    roff = hc * L + t4 * 512
                    ST["lrd"] = sch.ev("vector", lambda e, roff=roff, pdi=pdi: e.tensor_add(
                        resid[:, roff:roff + 512], resid[:, roff:roff + 512], pB[pdi][:]),
                        waits=[("pe", p_d), ("dve", ST["lrd"])], inc="dve")
                    consumed(reg, "dve")
                    yield 8
                h2cons = [("pe", sch.cnt["pe"])]

    def all_ffn():
        yield from shared_work()
        yield from expert_work()

    ffn_gen = all_ffn()
    ffn_alive = [True]
    SLOT = [0]
    FFN_MM = [0]
    TOT_MM = 768 + 3072

    def ffn_step():
        if ffn_alive[0]:
            try:
                FFN_MM[0] += next(ffn_gen)
            except StopIteration:
                ffn_alive[0] = False

    for g in range(8):
        ln_stats(g)
        if g >= 1:
            ln_finish(g - 1)
            gg = g - 1
            if gg >= 1:
                cap = min(64 * gg, 192 * ((gg + 1) // 2))
                while FFN_MM[0] < cap and ffn_alive[0]:
                    ffn_step()
        if g == 2:
            emit_p1_chain()
    ln_finish(7)
    ln_pe = sch.cnt["pe"]
    ln_act = sch.cnt["act"]
    ln_dve = sch.cnt["dve"]

    tap("xn", xn[:], [("dve", ln_dve)])
    tap("resid0", resid[:], [("act", ln_act)])
    if phases <= 0:
        return _finish(nc, sch)

    sch.ev("vector", lambda e: e.memset(Vaug[:, 64:2080:65], 1.0),
           waits=[("pe", ln_pe), ("act", ln_act)], inc="dve")

    PAIR_ORDER = [7, 6, 0, 1, 2, 3, 4, 5]
    wpair_load(PAIR_ORDER[0], 0, [])
    wpair_load(PAIR_ORDER[0], 1, [])
    eb_bc = h2t4[:, 0:2048]
    for q4e in range(4):
        sch.dma("gpsimd", eb_bc[:],
                AP(tdram, q4e * 4 * 512, [[0, 128], [1, 2048]]),
                waits=[("ebB", 16), ("ebD", 16 * q4e)], inc="ebC")
        sch.dma("gpsimd", AP(expdram, q4e * 4 * 81920, [[641, 128], [81920, 4], [1, 512]]),
                eb_bc.rearrange("p (h m) -> p h m", h=4),
                waits=[("ebC", 16 * (q4e + 1))], inc="ebD")
    sch.dma("gpsimd", eb_all[:, 11 * 384:16 * 384].rearrange("p (h j) -> p h j", h=5),
            AP(expdram, 127 + 11 * 81920, [[640, 128], [81920, 5], [1, 384]]),
            waits=[("ebD", 64)], inc="ebA")
    eb_readyA = ("ebA", sch.cnt["ebA"])
    eb_readyB = ("ebA", sch.cnt["ebA"] + 16)
    for ji, j in enumerate(PAIR_ORDER):
        pbanks = [2, 3, 6, 7] if ji == 0 else [0, 1, 2, 3]
        wpair_prep(j, [("pe", last_scores_pe)] if ji > 0 else [], pbanks)
        a_wp = sch.cnt["act"]
        for wi, (which, colbase, scale) in enumerate(
                (("q", 2048, 0.125), ("k", 0, None), ("v", 1024, None))):
            dstt = {"q": Qp, "k": Kp, "v": Vfm}[which]
            for t4 in range(4):
                bi = pbanks[t4 % len(pbanks)]
                reg = f"B{bi}"
                pdst = pB[bi]
                w0 = pe_w(reg) + [("act", a_wp), ("dve", ln_dve)]
                if which == "q":
                    w0 += [("pe", last_scores_pe)] if ji > 0 else []
                for dc in range(NCH):
                    sch.ev("tensor", lambda e, t4=t4, dc=dc, wi=wi, pdst=pdst: e.matmul(
                        pdst[:], w_pair[:, (wi * 8 + dc) * 128: (wi * 8 + dc + 1) * 128],
                        xn[:, dc * L + t4 * 512:dc * L + (t4 + 1) * 512],
                        start=(dc == 0), stop=(dc == NCH - 1)),
                        waits=(w0 if dc == 0 else []), inc="pe")
                pmm = sch.cnt["pe"]
                ew = [("pe", pmm)]
                if ji > 0:
                    ew.append(("pe", last_scores_pe))
                scap = hn_sb[:, 2:3] if scale is not None else hn_sb[:, 3:4]
                sch.ev("scalar", lambda e, t4=t4, pdst=pdst, dstt=dstt, scap=scap:
                       e.activation(dstt[:, t4 * 512:(t4 + 1) * 512], pdst[:],
                                    AF.Copy, scale=scap[:, 0:1]),
                       waits=ew + [consts], inc="act")
                consumed(reg, "act")
        for tc in range(NT):
            bi = pbanks[tc % len(pbanks)]
            reg = f"B{bi}"
            pb = pBb[bi]
            sch.ev("tensor", lambda e, tc=tc, pb=pb: e.transpose(
                pb[:, 0:128], Vfm[:, tc * 128:(tc + 1) * 128], id_bf[:]),
                waits=pe_w(reg) + [("act", sch.cnt["act"]), ("pe", last_vaug_pe)], inc="pe")
            p_tr = sch.cnt["pe"]
            dst = Vaug[:, tc * 130: (tc + 1) * 130].rearrange(
                "p (g c) -> p g c", g=2)[:, :, 0:64]
            src = pb[:, 0:128].rearrange("p (g c) -> p g c", g=2)
            sch.ev("vector", lambda e, dst=dst, src=src: e.tensor_copy(dst, src),
                   waits=[("pe", p_tr)], inc="dve")
            consumed(reg, "dve")
        proj_evac_dve = sch.cnt["dve"]
        proj_evac_act = sch.cnt["act"]
        if ji == 0:
            router_exp()
        if ji < 7:
            wpair_load(PAIR_ORDER[ji + 1], 0, [("pe", sch.cnt["pe"])])
            wpair_load(PAIR_ORDER[ji + 1], 1, [("pe", sch.cnt["pe"])])

        for q4 in range(4):
            p_sc = {}
            a_exp_t = {}

            def tail_exp(kc):
                ping = kc % 2
                for hh in range(2):
                    sb = 2 * hh + ping
                    ab = 2 * hh + ping
                    sch.ev("scalar", lambda e, sb=sb, ab=ab: e.activation(
                        attb[ab][:], pB[sb][:], AF.Exp),
                        waits=[("pe", p_sc[(kc, hh)])] + attb_cons[ab], inc="act")
                    consumed(f"B{sb}", "act")
                    a_exp_t[(kc, hh)] = sch.cnt["act"]

            av_w = {}

            def tail_eb(kc):
                ping = kc % 2
                for hh in range(2):
                    ab = 2 * hh + ping
                    a_exp = a_exp_t[(kc, hh)]
                    h = 2 * j + hh
                    qt0 = q4 * 4
                    bt = [qt for qt in range(max(qt0, kc - 1), min(qt0 + 4, kc + 2))]
                    w_av = [("act", a_exp)]
                    d_eb = None
                    if bt:
                        c0 = (bt[0] - qt0) * 128
                        c1 = (bt[-1] - qt0) * 128 + 128
                        e0 = (bt[0] - kc + 1) * 128
                        e1 = (bt[-1] - kc + 1) * 128 + 128
                        ebr = eb_readyA if h >= 11 else eb_readyB
                        sch.ev("vector", lambda e, ab=ab, h=h, c0=c0, c1=c1, e0=e0, e1=e1:
                               e.tensor_mul(attb[ab][:, c0:c1], attb[ab][:, c0:c1],
                                            eb_all[:, h * 384 + e0:h * 384 + e1]),
                               waits=[("act", a_exp), ebr], inc="dve")
                        d_eb = sch.cnt["dve"]
                        w_av = [("dve", d_eb)]
                    av_w[(kc, hh)] = (w_av, d_eb)

            def tail(kc):
                ping = kc % 2
                for hh in range(2):
                    ab = 2 * hh + ping
                    w_av, d_eb = av_w[(kc, hh)]
                    reg = "B4" if hh == 0 else "B5"
                    w2 = list(w_av)
                    if kc == 0:
                        w2 += avdone[reg]
                    sch.ev("tensor", lambda e, hh=hh, kc=kc, ab=ab: e.matmul(
                        pB[4 + hh][0:65, :],
                        Vaug[:, kc * 130 + hh * 65: kc * 130 + (hh + 1) * 65],
                        attb[ab][:],
                        start=(kc == 0), stop=(kc == NT - 1)),
                        waits=w2, inc="pe")
                    attb_cons[ab] = [("pe", sch.cnt["pe"])]
                    if d_eb is not None:
                        attb_cons[ab].append(("dve", d_eb))

            for kc in range(NT):
                ping = kc % 2
                w_pair2 = (pe_w(f"B{0 + ping}") + pe_w(f"B{2 + ping}")
                           + [("dve", proj_evac_dve), ("act", proj_evac_act)])
                for hh in range(2):
                    sb = 2 * hh + ping
                    base = 64 * hh
                    sch.ev("tensor", lambda e, kc=kc, q4=q4, sb=sb, base=base: e.matmul(
                        pB[sb][:], Kp[base:base + 64, kc * 128:(kc + 1) * 128],
                        Qp[base:base + 64, q4 * 512:(q4 + 1) * 512],
                        start=True, stop=True, tile_position=(base, 0)),
                        waits=(w_pair2 if hh == 0 else []), inc="pe")
                    p_sc[(kc, hh)] = sch.cnt["pe"]
                if kc >= 1:
                    tail_exp(kc - 1)
                    tail_eb(kc - 1)
                SLOT[0] += 1
                tgt = (TOT_MM * SLOT[0]) // 512
                while FFN_MM[0] < tgt and ffn_alive[0]:
                    ffn_step()
                if kc >= 1:
                    tail(kc - 1)
                if ji == 0 and q4 == 0:
                    router_chain_step(kc)
                    coefT_step()
            tail_exp(NT - 1)
            tail_eb(NT - 1)
            tail(NT - 1)
            last_vaug_pe = sch.cnt["pe"]
            p_av = sch.cnt["pe"]
            newpend = []
            for hh in range(2):
                reg = "B4" if hh == 0 else "B5"
                slot = 2 * hh + (q4 % 2)
                sch.ev("scalar", lambda e, hh=hh, slot=slot: e.copy(
                    av_ep[slot][:], pB[4 + hh][0:65, :]),
                    waits=[("pe", p_av)] + av_ep_cons[slot], inc="act")
                avdone[reg] = [("act", sch.cnt["act"])]
                newpend.append({"hh": hh, "slot": slot, "q4": q4, "h": 2 * j + hh,
                                "evac": sch.cnt["act"]})
            epi_stage2()
            pend = newpend
            epi_stage1()
        last_scores_pe = sch.cnt["pe"]

    epi_stage2()
    pend = []
    while ffn_alive[0]:
        ffn_step()

    attn_dve = sch.cnt["dve"]
    attn_act = sch.cnt["act"]
    tap("resid_attn", resid[:], [("dve", attn_dve)])
    if phases <= 4:
        return _finish(nc, sch)

    tap("resid_sh", resid[:], [("dve", sch.cnt["dve"])])

    tap("resid_moe", resid[:], [("dve", sch.cnt["dve"])])
    if phases <= 7:
        return _finish(nc, sch)

    outst = [AR_M[:, 0:2048], AR_M[:, 2048:4096]]
    p7_dve = sch.cnt["dve"]
    out_cons = [[("pe", sch.cnt["pe"]), ("dve", p7_dve)],
                [("pe", sch.cnt["pe"]), ("dve", p7_dve)]]
    for g in range(8):
        sl = g % 2
        for ti in range(2):
            tc = 2 * g + ti
            for half in range(2):
                reg = ["B6", "B7"][(2 * ti + half) % 2]
                pb = pB[6 + (2 * ti + half) % 2][:]
                dcs = [half * 4 + i for i in range(4)]
                ws = pe_w(reg) + [("dve", p7_dve)]
                for i, dc in enumerate(dcs):
                    sch.ev("tensor", lambda e, i=i, dc=dc, tc=tc, pb=pb: e.transpose(
                        pb[:, i * 128:(i + 1) * 128],
                        resid[:, dc * L + tc * 128:dc * L + (tc + 1) * 128], id_f32[:]),
                        waits=(ws if i == 0 else []), inc="pe")
                p_t = sch.cnt["pe"]
                sch.ev("vector", lambda e, sl=sl, ti=ti, half=half, pb=pb: e.tensor_copy(
                    outst[sl][:, ti * 1024 + half * 512: ti * 1024 + (half + 1) * 512], pb[:]),
                    waits=[("pe", p_t)] + (out_cons[sl] if ti == 0 and half == 0 else []),
                    inc="dve")
                consumed(reg, "dve")
        d_o = sch.cnt["dve"]
        sch.dma("sync", AP(y_out, g * 256 * D, [[D, 128], [128 * D, 2], [1, D]]),
                outst[sl].rearrange("p (t c) -> p t c", t=2),
                waits=[("dve", d_o)], inc="outq")
        out_cons[sl] = [("outq", sch.cnt["outq"])]
    sch.ev("sync", lambda e: e.nop(), waits=[("outq", sch.cnt["outq"])])

    return _finish(nc, sch)


def _finish(nc, sch):
    with nc.allow_low_precision(reason="bf16 score writeout by design"):
        with nc.Block() as block:
            sch.emit(block)
    return nc


def _shard_inputs(inputs):
    x = np.ascontiguousarray(inputs["L0_attn_x"], dtype=np.float32)
    common = {
        "w_in": np.ascontiguousarray(inputs["L0_attn_w_in"], dtype=np.float32),
        "k_off": np.ascontiguousarray(inputs["L0_attn_kernel_offsets"], dtype=np.float32),
        "k_amp": np.ascontiguousarray(inputs["L0_attn_kernel_amplitudes"], dtype=np.float32),
        "k_sh": np.ascontiguousarray(inputs["L0_attn_kernel_sharpness"], dtype=np.float32),
        "w_rtr": np.ascontiguousarray(inputs["L0_ffn_router"], dtype=np.float32),
        "w_eg": np.ascontiguousarray(inputs["L0_ffn_experts_gate"], dtype=np.float32),
        "w_eu": np.ascontiguousarray(inputs["L0_ffn_experts_up"], dtype=np.float32),
        "w_ed": np.ascontiguousarray(inputs["L0_ffn_experts_down"], dtype=np.float32),
        "w_sg": np.ascontiguousarray(inputs["L0_ffn_shared_gate"], dtype=np.float32),
        "w_su": np.ascontiguousarray(inputs["L0_ffn_shared_up"], dtype=np.float32),
        "w_sd": np.ascontiguousarray(inputs["L0_ffn_shared_down"], dtype=np.float32),
    }
    return [dict(common, x=np.ascontiguousarray(x[b])) for b in range(B)]


def kernel(**inputs):
    nc = build_program()
    in_maps = _shard_inputs(inputs)
    res = run_bass_kernel_spmd(nc, in_maps, core_ids=list(range(B)))
    return np.stack([r["y"] for r in res.results], axis=0).astype(np.float32)


if __name__ == "__main__":
    nc = build_program(phases=0)
    print("build ok; instructions:", nc._state().n_instructions()
          if hasattr(nc, "_state") else "?")


# version 54
# speedup vs baseline: 1.5431x; 1.0137x over previous
import sys

sys.path.insert(0, "/opt/trn_rl_repo")

import numpy as np
import ml_dtypes

import concourse.bass as bass
import concourse.mybir as mybir
from concourse.ap import AP
from concourse.bass_utils import run_bass_kernel_spmd

F32 = mybir.dt.float32
BF16 = mybir.dt.bfloat16
AF = mybir.ActivationFunctionType
ALU = mybir.AluOpType
AX = mybir.AxisListType

B, L, D = 8, 2048, 1024
H = 16
NE, DEXP = 8, 512
NCH = 8
NT = 16
EPS = 1e-5


class Sched:

    ENGINES = ("sync", "gpsimd", "scalar", "vector", "tensor")

    def __init__(self, nc):
        self.nc = nc
        self.events = {e: [] for e in self.ENGINES}
        self.sems = {}
        self.cnt = {}

    def sem(self, name):
        if name not in self.sems:
            self.sems[name] = self.nc.alloc_semaphore(name)
            self.cnt[name] = 0
        return name

    def ev(self, eng, fn, waits=(), inc=None, amt=1):
        ws = []
        for s, v in waits:
            assert s in self.cnt, f"wait on unknown sem {s}"
            assert v <= self.cnt[s], f"wait {s}>={v} but only {self.cnt[s]} scheduled"
            if v > 0:
                ws.append((s, v))
        self.events[eng].append((fn, tuple(ws), inc, amt))
        if inc is not None:
            self.sem(inc)
            self.cnt[inc] += amt
            return self.cnt[inc]
        return None

    def dma(self, eng, out, in_, waits=(), inc=None):
        return self.ev(eng, lambda e: e.dma_start(out=out, in_=in_),
                       waits=waits, inc=inc, amt=16)

    def emit(self, block):
        def mkbody(name):
            evs = self.events[name]
            sems = self.sems

            def body(eng):
                guaranteed = {}
                for fn, waits, inc, amt in evs:
                    for s, v in waits:
                        if guaranteed.get(s, 0) < v:
                            eng.wait_ge(sems[s], v)
                            guaranteed[s] = v
                    ins = fn(eng)
                    if inc is not None:
                        ins.then_inc(sems[inc], amt)
            return body

        block.sync(mkbody("sync"))
        block.gpsimd(mkbody("gpsimd"))
        block.scalar(mkbody("scalar"))
        block.vector(mkbody("vector"))
        block.tensor(mkbody("tensor"))


def build_program(phases=99, taps=()):
    nc = bass.Bass("TRN2", target_bir_lowering=False)
    sch = Sched(nc)
    for s in ("pe", "act", "dve", "misc",
              "x0", "x1", "wst0", "wst1", "wst2", "wst3",
              "ebA", "ebB", "ebC", "ebD", "cbw",
              "wg0", "wu0", "wd0", "wg1", "wu1", "wd1",
              "cb0", "cb1", "rtrw", "shw", "outq"):
        sch.sem(s)

    x_in = nc.dram_tensor("x", [L, D], F32, kind="ExternalInput")
    w_in = nc.dram_tensor("w_in", [3 * D, D], F32, kind="ExternalInput")
    k_off = nc.dram_tensor("k_off", [H, 16], F32, kind="ExternalInput")
    k_amp = nc.dram_tensor("k_amp", [H, 16], F32, kind="ExternalInput")
    k_sh = nc.dram_tensor("k_sh", [H, 16], F32, kind="ExternalInput")
    w_rtr = nc.dram_tensor("w_rtr", [D, NE], F32, kind="ExternalInput")
    w_eg = nc.dram_tensor("w_eg", [NE, D, DEXP], F32, kind="ExternalInput")
    w_eu = nc.dram_tensor("w_eu", [NE, D, DEXP], F32, kind="ExternalInput")
    w_ed = nc.dram_tensor("w_ed", [NE, DEXP, D], F32, kind="ExternalInput")
    w_sg = nc.dram_tensor("w_sg", [D, D], F32, kind="ExternalInput")
    w_su = nc.dram_tensor("w_su", [D, D], F32, kind="ExternalInput")
    w_sd = nc.dram_tensor("w_sd", [D, D], F32, kind="ExternalInput")
    y_out = nc.dram_tensor("y", [L, D], F32, kind="ExternalOutput")

    tdram = nc.dram_tensor("tdram", [16 * 512], F32, kind="Internal")
    expdram = nc.dram_tensor("expdram", [16 * 128 * 640 + 640], F32, kind="Internal")
    coefdram = nc.dram_tensor("coefdram", [NE * L], F32, kind="Internal")

    ident_bf = nc.inline_tensor(np.eye(128, dtype=ml_dtypes.bfloat16), "ident_bf")
    ident_f32 = nc.inline_tensor(np.eye(128, dtype=np.float32), "ident_f32")
    ones64 = nc.inline_tensor(np.ones((1, 64), dtype=ml_dtypes.bfloat16), "ones64c")
    hn_c = nc.inline_tensor(np.concatenate(
        [np.full((128, 1), 0.5, dtype=np.float32),
         np.full((128, 1), -1.0, dtype=np.float32),
         np.full((128, 1), 0.125, dtype=np.float32),
         np.full((128, 1), 1.0, dtype=np.float32)], axis=1), "hn_c")
    drow = nc.inline_tensor((np.arange(512, dtype=np.float32) - 255.0)[None, :], "drow")
    sel = np.zeros((2, 128, 16), dtype=np.float32)
    for hb in range(2):
        for p in range(128):
            sel[hb, p, hb * 8 + p // 16] = 1.0
    sel_c = nc.inline_tensor(np.ascontiguousarray(sel.reshape(2 * 128, 16)), "sel_c")

    resid = nc.alloc_sbuf_tensor("resid", [128, NCH * L], F32)
    xn = nc.alloc_sbuf_tensor("xn", [128, NCH * L], BF16)
    id_bf = nc.alloc_sbuf_tensor("id_bf", [128, 128], BF16)
    id_f32 = nc.alloc_sbuf_tensor("id_f32", [128, 128], F32)
    ones_sb = nc.alloc_sbuf_tensor("ones_sb", [128, 64], BF16)
    sel_sb = nc.alloc_sbuf_tensor("sel_sb", [128, 32], F32)
    rtr_sb = nc.alloc_sbuf_tensor("rtr_sb", [128, NCH * NE], BF16)
    hn_sb = nc.alloc_sbuf_tensor("hn_sb", [128, 4], F32)
    AR_W = nc.alloc_sbuf_tensor("AR_W", [128, 12288], F32)
    AR_M = nc.alloc_sbuf_tensor("AR_M", [128, 7952], F32)
    AR_E = nc.alloc_sbuf_tensor("AR_E", [128, 3072], F32)
    w_pair = nc.alloc_sbuf_tensor("w_pair", [128, 3072], BF16)
    wstage = nc.alloc_sbuf_tensor("wstage", [128, 2048], BF16)
    h2t4 = nc.alloc_sbuf_tensor("h2t4", [128, 2048], F32)

    pB = [nc.alloc_psum_tensor(f"pb{i}", [128, 512], F32) for i in range(8)]
    pBb = [pB[i][:, 0:512].bitcast(BF16) for i in range(8)]

    regw = {f"B{i}": [] for i in range(8)}

    def pe_w(*regions):
        ws = []
        for r in regions:
            ws += regw[r]
        return ws

    def consumed(region, sem):
        regw[region] = [(sem, sch.cnt[sem])]

    tapdefs = []
    ST = {"lrd": 0}

    def tap(name, ap, waits):
        if name in taps:
            th = nc.dram_tensor("tap_" + name, list(ap.shape), ap.dtype,
                                kind="ExternalOutput")
            sch.dma("sync", th[:], ap, waits=waits, inc="misc")

    sch.dma("sync", id_bf[:], ident_bf[:], inc="misc")
    sch.dma("sync", id_f32[:], ident_f32[:], inc="misc")
    sch.dma("sync", ones_sb[:], AP(ones64, 0, [[0, 128], [1, 64]]), inc="misc")
    sch.dma("sync", sel_sb[:, 0:16], AP(sel_c, 0, [[16, 128], [1, 16]]), inc="misc")
    sch.dma("sync", sel_sb[:, 16:32], AP(sel_c, 128 * 16, [[16, 128], [1, 16]]), inc="misc")
    sch.dma("sync", hn_sb[:], hn_c[:], inc="misc")
    consts = ("misc", sch.cnt["misc"])

    tb_cols = AR_M[:, 6160:6176]
    tb_selamp = AR_M[:, 6176:6192]
    tb_dbc = AR_M[:, 6192:6704]
    tb_tmp = h2t4[:, 0:512]
    tb_ebt = AR_M[:, 7216:7728]
    eb_all = AR_E[:, 0:3072].bitcast(BF16)

    sch.dma("scalar", tb_dbc[:], AP(drow, 0, [[0, 128], [1, 512]]), inc="ebA")
    for hb in range(2):
        base = hb * 128
        sch.dma("scalar", tb_cols[:, 0 + hb:1 + hb], AP(k_off, base, [[1, 128], [1, 1]]), inc="ebA")
        sch.dma("scalar", tb_cols[:, 2 + hb:3 + hb], AP(k_amp, base, [[1, 128], [1, 1]]), inc="ebA")
        sch.dma("scalar", tb_cols[:, 4 + hb:5 + hb], AP(k_sh, base, [[1, 128], [1, 1]]), inc="ebA")
    ebl = ("ebA", sch.cnt["ebA"])

    tb_tmp2 = h2t4[:, 512:1024]

    def emit_p1_chain():
      prev_half = []
      for hb in range(2):
        oc = tb_cols[:, 0 + hb:1 + hb]
        ac = tb_cols[:, 2 + hb:3 + hb]
        sc = tb_cols[:, 4 + hb:5 + hb]
        d_sub = sch.ev("vector", lambda e, oc=oc: e.tensor_scalar(
            tb_tmp[:], tb_dbc[:], oc, None, ALU.subtract),
            waits=[ebl] + prev_half, inc="dve")
        d_sq = sch.ev("vector", lambda e: e.tensor_mul(tb_tmp[:], tb_tmp[:], tb_tmp[:]),
                      waits=[("dve", d_sub)], inc="dve")
        d_t = sch.ev("vector", lambda e, sc=sc: e.tensor_scalar(
            tb_tmp[:], tb_tmp[:], sc, None, ALU.mult), waits=[("dve", d_sq)], inc="dve")
        d_n = sch.ev("vector", lambda e: e.tensor_scalar(
            tb_tmp2[:], tb_tmp[:], -1.0, None, ALU.mult), waits=[("dve", d_t)], inc="dve")
        d3 = sch.ev("vector", lambda e: e.tensor_tensor(
            tb_tmp[:], tb_tmp[:], tb_tmp2[:], ALU.min), waits=[("dve", d_n)], inc="dve")
        sch.ev("scalar", lambda e: e.activation(tb_tmp[:], tb_tmp[:], AF.Exp),
               waits=[("dve", d3)], inc="act")
        a3 = sch.cnt["act"]
        sch.ev("vector", lambda e, hb=hb, ac=ac: e.tensor_scalar(
            tb_selamp[:], sel_sb[:, hb * 16:(hb + 1) * 16], ac, None, ALU.mult),
            waits=[consts], inc="dve")
        d4 = sch.cnt["dve"]
        sch.ev("tensor", lambda e, hb=hb: e.matmul(pB[1][0:16, :], tb_selamp[:], tb_tmp[:],
                                                   start=(hb == 0), stop=(hb == 1)),
               waits=pe_w("B1") + [("act", a3), ("dve", d4)], inc="pe")
        prev_half = [("act", a3), ("pe", sch.cnt["pe"])]
      p_T = sch.cnt["pe"]
      tap("tb_tmp", tb_tmp[:], [("act", sch.cnt["act"])])
      tap("tb_selamp", tb_selamp[:], [("dve", sch.cnt["dve"])])
      tap("tb_cols", tb_cols[:], [("dve", sch.cnt["dve"])])
      sch.ev("scalar", lambda e: e.activation(tb_ebt[0:16, :], pB[1][0:16, :], AF.Exp),
           waits=[("pe", p_T)], inc="act")
      tap("tb_ebt", tb_ebt[:], [("act", sch.cnt["act"])])
      consumed("B1", "act")
      a_ebt = sch.cnt["act"]
      sch.dma("scalar", AP(tdram, 0, [[512, 16], [1, 512]]), tb_ebt[0:16, :],
            waits=[("act", a_ebt)], inc="ebB")


    if phases <= 1:
        return _finish(nc, sch)

    coef_all = AR_M[:, 5632:5760]
    coef_scr = AR_M[:, 5760:5792]
    e8_all = AR_M[:, 5120:5248]
    coefT_sb = h2t4[0:8, 0:2048]
    coef_wr = None
    rtab = {"d_c": [None] * NT, "done": 0, "p_all": 0}

    sch.dma("gpsimd", rtr_sb.rearrange("p (c e) -> p c e", c=NCH),
            AP(w_rtr, 0, [[NE, 128], [128 * NE, NCH], [1, NE]]),
            inc="rtrw")
    rtr_rd = ("rtrw", sch.cnt["rtrw"])

    def router_exp():
        sch.ev("scalar", lambda e: e.activation(e8_all[:], pB[0][:, 0:128], AF.Exp),
               waits=[("pe", rtab["p_all"])], inc="act")
        consumed("B0", "act")
        rtab["a_e"] = sch.cnt["act"]

    def router_chain_step(tc):
        st4 = coef_scr[:, 8:12]
        probs = coef_scr[:, 12:20]
        mask = coef_scr[:, 20:28]
        if True:
            e8 = e8_all[:, tc * 8:(tc + 1) * 8]
            dd = sch.ev("vector", lambda e, e8=e8: e.reduce_sum(
                st4[:, 0:1], e8, axis=AX.X),
                waits=([("act", rtab["a_e"])] if tc == 0 else []), inc="dve")
            dd = sch.ev("vector", lambda e: e.reciprocal(st4[:, 1:2], st4[:, 0:1]),
                        waits=[("dve", dd)], inc="dve")
            dd = sch.ev("vector", lambda e, e8=e8: e.tensor_scalar(
                probs, e8, st4[:, 1:2], None, ALU.mult),
                waits=[("dve", dd)], inc="dve")
            dd = sch.ev("vector", lambda e: e.reduce_max(st4[:, 2:3], probs, axis=AX.X),
                        waits=[("dve", dd)], inc="dve")
            dd = sch.ev("vector", lambda e: e.tensor_scalar(
                mask, probs, st4[:, 2:3], -1e30, ALU.is_equal, ALU.mult),
                waits=[("dve", dd)], inc="dve")
            dd = sch.ev("vector", lambda e: e.tensor_add(mask, mask, probs),
                        waits=[("dve", dd)], inc="dve")
            dd = sch.ev("vector", lambda e: e.reduce_max(st4[:, 3:4], mask, axis=AX.X),
                        waits=[("dve", dd)], inc="dve")
            dd = sch.ev("vector", lambda e: e.tensor_scalar(
                mask, probs, st4[:, 3:4], None, ALU.is_ge),
                waits=[("dve", dd)], inc="dve")
            dd = sch.ev("vector", lambda e, tc=tc: e.tensor_mul(
                coef_all[:, tc * 8:(tc + 1) * 8], mask, probs),
                waits=[("dve", dd)], inc="dve")
            rtab["d_c"][tc] = sch.cnt["dve"]
            if tc == NT - 1:
                av_ep_cons[0] = [("dve", sch.cnt["dve"])]

    def coefT_step():
        tc = rtab["done"]
        if tc >= NT:
            return
        rtab["done"] += 1
        sch.ev("tensor", lambda e, tc=tc: e.transpose(
            pB[7][0:8, 0:128], coef_all[:, tc * 8:(tc + 1) * 8], id_f32[:]),
            waits=pe_w("B7") + [("dve", rtab["d_c"][tc])], inc="pe")
        p_t = sch.cnt["pe"]
        sch.ev("vector", lambda e, tc=tc: e.tensor_copy(
            coefT_sb[:, tc * 128:(tc + 1) * 128], pB[7][0:8, 0:128]),
            waits=[("pe", p_t)] + ([("ebD", 64)] if tc == 0 else []), inc="dve")
        consumed("B7", "dve")
        if tc == NT - 1:
            nonlocal coef_wr
            d_ct = sch.cnt["dve"]
            sch.dma("scalar", AP(coefdram, 0, [[2048, 8], [1, 2048]]), coefT_sb[:],
                    waits=[("dve", d_ct)], inc="cbw")
            coef_wr = ("cbw", sch.cnt["cbw"])
            av_ep_cons[1] = [("pe", p_t)]
            av_ep_cons[2] = [("pe", p_t)]


    stage = [AR_M[:, 0:2048], AR_M[:, 2048:4096]]
    xn_tm = [[AR_M[:, 4096:4608].bitcast(BF16), AR_M[:, 4608:5120].bitcast(BF16)],
             [AR_M[:, 5120:5632].bitcast(BF16), AR_M[:, 5632:6144].bitcast(BF16)]]
    sq_scr_p = [AR_M[:, 4096:5120], AR_M[:, 5120:6144]]
    lncol = AR_M[:, 6144:6160]

    xsem = ["x0", "x1"]
    xfree = [[], []]
    ln_prog = []
    xv_p = {}
    a_xn_p = {}

    def ln_stats(g):
        sl = g % 2
        xv = sch.dma("sync" if sl == 0 else "scalar",
                     stage[sl].rearrange("p (t c) -> p t c", t=2),
                     AP(x_in, g * 256 * D, [[D, 128], [128 * D, 2], [1, D]]),
                     waits=xfree[sl], inc=xsem[sl])
        xv_p[g] = xv
        a0l = 0
        for ti in range(2):
            xt = stage[sl][:, ti * 1024:(ti + 1) * 1024]
            sch.ev("vector", lambda e, xt=xt, ti=ti: e.reduce_sum(
                lncol[:, ti:ti + 1], xt, axis=AX.X), waits=[(xsem[sl], xv)], inc="dve")
            a0l = sch.ev("scalar", lambda e, xt=xt, ti=ti, sl=sl: e.activation(
                sq_scr_p[sl][:], xt, AF.Square, accum_out=lncol[:, 2 + ti:3 + ti]),
                waits=[(xsem[sl], xv)], inc="act")
        d_s = sch.cnt["dve"]
        d_nm = sch.ev("vector", lambda e: e.tensor_scalar(
            lncol[:, 4:6], lncol[:, 0:2], -1.0 / D, None, ALU.mult),
            waits=[("dve", d_s)], inc="dve")
        d_m2 = sch.ev("vector", lambda e: e.tensor_scalar(
            lncol[:, 6:8], lncol[:, 2:4], 1.0 / D, None, ALU.mult),
            waits=[("act", a0l)], inc="dve")
        d_nq = sch.ev("vector", lambda e: e.scalar_tensor_tensor(
            lncol[:, 8:10], lncol[:, 4:6], -1.0, lncol[:, 4:6],
            ALU.mult, ALU.mult), waits=[("dve", d_nm)], inc="dve")
        d1 = sch.ev("vector", lambda e: e.scalar_tensor_tensor(
            lncol[:, 8:10], lncol[:, 8:10], EPS, lncol[:, 6:8],
            ALU.add, ALU.add), waits=[("dve", d_nq), ("dve", d_m2)], inc="dve")
        a1h = sch.ev("scalar", lambda e: e.activation(
            lncol[:, 10:12], lncol[:, 8:10], AF.Ln),
            waits=[("dve", d1)], inc="act")
        d1h = sch.ev("vector", lambda e: e.tensor_scalar(
            lncol[:, 10:12], lncol[:, 10:12], -0.5, None, ALU.mult),
            waits=[("act", a1h)], inc="dve")
        a1 = sch.ev("scalar", lambda e: e.activation(
            lncol[:, 12:14], lncol[:, 10:12], AF.Exp),
            waits=[("dve", d1h)], inc="act")
        a_xn = []
        for ti in range(2):
            xt = stage[sl][:, ti * 1024:(ti + 1) * 1024]
            sch.ev("vector", lambda e, xt=xt, sl=sl, ti=ti: e.tensor_scalar(
                xn_tm[sl][ti][:], xt, lncol[:, 4 + ti:5 + ti],
                lncol[:, 12 + ti:13 + ti], ALU.add, ALU.mult),
                waits=[("act", a1), ("dve", d_nm)], inc="dve")
            a_xn.append(("dve", sch.cnt["dve"]))
        a_xn_p[g] = a_xn

    def ln_finish(g):
        sl = g % 2
        xv = xv_p[g]
        a_xn = a_xn_p[g]
        for dc in range(NCH):
            rb = ["B6", "B7"][dc % 2]
            rf = ["B4", "B5"][dc % 2]
            pb = pBb[6 + dc % 2][:, 0:256]
            pf = pB[4 + dc % 2][:, 0:256]
            for ti in range(2):
                sch.ev("tensor", lambda e, ti=ti, dc=dc, pb=pb, sl=sl: e.transpose(
                    pb[:, ti * 128:(ti + 1) * 128],
                    xn_tm[sl][ti][:, dc * 128:(dc + 1) * 128], id_bf[:]),
                    waits=pe_w(rb) + [a_xn[ti], consts], inc="pe")
                sch.ev("tensor", lambda e, ti=ti, dc=dc, pf=pf, sl=sl: e.transpose(
                    pf[:, ti * 128:(ti + 1) * 128],
                    stage[sl][:, ti * 1024 + dc * 128: ti * 1024 + (dc + 1) * 128],
                    id_f32[:]),
                    waits=pe_w(rf) + [(xsem[sl], xv)], inc="pe")
            p_tr = sch.cnt["pe"]
            sch.ev("vector", lambda e, dc=dc, g=g, pb=pb: e.tensor_copy(
                xn[:, dc * L + g * 256: dc * L + g * 256 + 256], pb[:]),
                waits=[("pe", p_tr)], inc="dve")
            consumed(rb, "dve")
            sch.ev("scalar", lambda e, dc=dc, g=g, pf=pf: e.copy(
                resid[:, dc * L + g * 256: dc * L + g * 256 + 256], pf[:]),
                waits=[("pe", p_tr)], inc="act")
            consumed(rf, "act")
        xfree[sl] = [("pe", sch.cnt["pe"]), a_xn[1]]
        ln_prog.append((sch.cnt["dve"], sch.cnt["act"]))
        for tc2 in (2 * g, 2 * g + 1):
            w0r = (pe_w("B0") + [rtr_rd] if tc2 == 0 else []) + \
                  [("dve", ln_prog[g][0])]
            for dc in range(NCH):
                sch.ev("tensor", lambda e, tc2=tc2, dc=dc: e.matmul(
                    pB[0][:, tc2 * 8:(tc2 + 1) * 8],
                    xn[:, dc * L + tc2 * 128:dc * L + (tc2 + 1) * 128],
                    rtr_sb[:, dc * 8:(dc + 1) * 8],
                    start=(dc == 0), stop=(dc == NCH - 1)),
                    waits=(w0r if dc == 0 else []), inc="pe")
        if g == 7:
            rtab["p_all"] = sch.cnt["pe"]

    Wg_sh = AR_W[:, 0:4096].bitcast(BF16)
    Wu_sh = AR_W[:, 4096:8192].bitcast(BF16)
    Wd_sh = AR_W[:, 8192:12288].bitcast(BF16)
    h_s = AR_E[:, 0:2048].bitcast(BF16)
    gs = AR_M[:, 6928:7184].bitcast(BF16)

    wpat = [[D, 128], [128 * D, NCH], [1, D]]
    sch.dma("gpsimd", Wg_sh.rearrange("p (c n) -> p c n", c=NCH), AP(w_sg, 0, wpat),
            inc="shw")
    sch.dma("gpsimd", Wu_sh.rearrange("p (c n) -> p c n", c=NCH), AP(w_su, 0, wpat),
            inc="shw")
    sch.dma("gpsimd", Wd_sh.rearrange("p (c n) -> p c n", c=NCH), AP(w_sd, 0, wpat),
            inc="shw")
    sh_rd = ("shw", sch.cnt["shw"])


    wsem = ["wst0", "wst1", "wst2"]
    wst_v = [0, 0, 0]
    wst_slot = {0: 0, 1: 1, 2: 0}
    ROWB = {0: 2048, 1: 0, 2: 1024}

    def wpair_load(j, which, waits):
        sl = wst_slot[which]
        wst_v[which] = sch.dma(
            "gpsimd", wstage[:, sl * 1024:(sl + 1) * 1024],
            AP(w_in, (ROWB[which] + j * 128) * D, [[D, 128], [1, D]]),
            waits=waits, inc=wsem[which])

    def wpair_prep(j, extra, banks):
        for which in range(3):
            sl = wst_slot[which]
            if which == 2:
                wpair_load(j, 2, [("pe", sch.cnt["pe"])])
            for g2 in range(2):
                bi = banks[(2 * which + g2) % len(banks)]
                reg = f"B{bi}"
                pb = pBb[bi][:, 0:512]
                for i in range(4):
                    dc = g2 * 4 + i
                    sch.ev("tensor", lambda e, which=which, dc=dc, i=i, pb=pb, sl=sl: e.transpose(
                        pb[:, i * 128:(i + 1) * 128],
                        wstage[:, sl * 1024 + dc * 128: sl * 1024 + (dc + 1) * 128],
                        id_bf[:]),
                        waits=(pe_w(reg) + [(wsem[which], wst_v[which]), consts] + extra
                               if i == 0 else []), inc="pe")
                p_tr = sch.cnt["pe"]
                sch.ev("scalar", lambda e, which=which, g2=g2, pb=pb: e.copy(
                    w_pair[:, (which * 8 + g2 * 4) * 128: (which * 8 + g2 * 4 + 4) * 128],
                    pb[:]),
                    waits=[("pe", p_tr)], inc="act")
                consumed(reg, "act")

    if phases <= 2:
        return _finish(nc, sch)

    Qp = AR_M[:, 0:1024].bitcast(BF16)
    Kp = AR_M[:, 1024:2048].bitcast(BF16)
    Vaug = AR_M[:, 2048:3088].bitcast(BF16)
    Vfm = AR_M[:, 3088:4112].bitcast(BF16)
    attb = [AR_M[:, 4112 + i * 256: 4368 + i * 256].bitcast(BF16) for i in range(4)]
    rb_sb = AR_M[0:64, 6672:6928].bitcast(BF16)
    tmp_ep = AR_M[:, 7184:7440].bitcast(BF16)
    rrow = [AR_M[0:1, 7440:7696].bitcast(BF16),
            AR_M[0:1, 7696:7952].bitcast(BF16)]
    av_ep = [AR_M[0:65, 5136 + i * 256: 5392 + i * 256].bitcast(BF16)
             for i in range(4)]

    last_scores_pe = 0
    last_vaug_pe = 0
    proj_evac_dve = 0
    attb_cons = [[] for _ in range(4)]
    avdone = {"B4": list(regw["B4"]), "B5": list(regw["B5"])}
    av_ep_cons = [[], [], [], []]
    rrow_cons = [[], []]
    rb_sb_cons = []
    pend = []

    def epi_stage1():
        for P in pend:
            sch.ev("scalar", lambda e, P=P: e.activation(
                rrow[P["hh"]][:], av_ep[P["slot"]][64:65, :], AF.Ln),
                waits=[P["evac"], consts] + rrow_cons[P["hh"]], inc="act")
            P["d_r"] = sch.ev("scalar", lambda e, P=P: e.activation(
                rrow[P["hh"]][:], rrow[P["hh"]][:], AF.Exp,
                scale=hn_sb[0:1, 1:2]),
                waits=[("act", sch.cnt["act"])], inc="act")

    def epi_stage2():
        for P in pend:
            hh, q4p, h = P["hh"], P["q4"], P["h"]
            sch.ev("tensor", lambda e, P=P: e.matmul(
                pB[7][0:64, :], ones_sb[0:1, :], rrow[P["hh"]][:],
                start=True, stop=True),
                waits=pe_w("B7") + [("act", P["d_r"])], inc="pe")
            rrow_cons[hh] = [("pe", sch.cnt["pe"])]
            p_rb = sch.cnt["pe"]
            sch.ev("scalar", lambda e: e.copy(rb_sb[:], pB[7][0:64, :]),
                   waits=[("pe", p_rb)] + rb_sb_cons, inc="act")
            consumed("B7", "act")
            a_rb = sch.cnt["act"]
            rbase = (h % 2) * 64
            d_m = sch.ev("vector", lambda e, P=P, rbase=rbase: e.tensor_mul(
                tmp_ep[rbase:rbase + 64, :], av_ep[P["slot"]][0:64, :], rb_sb[:]),
                waits=[("act", a_rb)], inc="dve")
            rb_sb_cons[:] = [("dve", d_m)]
            roff = (h // 2) * L + q4p * 512
            ST["lrd"] = sch.ev(
                "vector", lambda e, rbase=rbase, roff=roff: e.tensor_add(
                    resid[rbase:rbase + 64, roff:roff + 512],
                    resid[rbase:rbase + 64, roff:roff + 512],
                    tmp_ep[rbase:rbase + 64, :]),
                waits=[("dve", d_m), ("dve", ST["lrd"])], inc="dve")
            av_ep_cons[P["slot"]] = [("dve", sch.cnt["dve"])]


    cbs = [AR_M[:, 6160:6416].bitcast(BF16),
           AR_M[:, 6416:6672].bitcast(BF16)]
    gs = AR_M[:, 6928:7184].bitcast(BF16)
    h2v = h2t4[:, 0:2048].bitcast(BF16)

    exp_area = {}
    WV_G0 = [0, 0]
    WV_U0 = [0, 0]
    WV_D0 = [0, 0]
    cb_v = [0, 0]
    cb_cons = [[], []]
    cb_seq = [(g, t, s2) for g in range(4) for t in range(4) for s2 in range(2)]
    egpat0 = [[DEXP, 128], [128 * DEXP, NCH], [1, DEXP]]
    edpat0 = [[D, 128], [128 * D, 4], [1, D]]

    def emit_cb(k):
        if k >= len(cb_seq):
            return
        g, t, s2 = cb_seq[k]
        ex = 2 * g + s2
        sl = k % 2
        assert coef_wr is not None
        cb_v[sl] = sch.dma("gpsimd", cbs[sl][:],
                           AP(coefdram, ex * L + t * 512, [[0, 128], [1, 512]]),
                           waits=[coef_wr] + cb_cons[sl], inc=f"cb{sl}")

    def shared_work():
        gs_cons2 = []
        hs_cons = []
        for t4 in range(4):
            for dsc in range(NCH):
                w0 = pe_w("B6") + [sh_rd, ("dve", ln_prog[2 * t4 + 1][0])]
                for hc in range(NCH):
                    sch.ev("tensor", lambda e, t4=t4, dsc=dsc, hc=hc: e.matmul(
                        pB[6][:], Wg_sh[:, hc * 1024 + dsc * 128: hc * 1024 + (dsc + 1) * 128],
                        xn[:, hc * L + t4 * 512: hc * L + (t4 + 1) * 512],
                        start=(hc == 0), stop=(hc == NCH - 1)),
                        waits=(w0 if hc == 0 else []), inc="pe")
                p_g = sch.cnt["pe"]
                w1 = pe_w("B7")
                for hc in range(NCH):
                    sch.ev("tensor", lambda e, t4=t4, dsc=dsc, hc=hc: e.matmul(
                        pB[7][:], Wu_sh[:, hc * 1024 + dsc * 128: hc * 1024 + (dsc + 1) * 128],
                        xn[:, hc * L + t4 * 512: hc * L + (t4 + 1) * 512],
                        start=(hc == 0), stop=(hc == NCH - 1)),
                        waits=(w1 if hc == 0 else []), inc="pe")
                p_u = sch.cnt["pe"]
                sch.ev("scalar", lambda e: e.activation(gs[:], pB[6][:], AF.Tanh,
                                                        scale=hn_sb[:, 0:1]),
                       waits=[("pe", p_g), consts] + gs_cons2, inc="act")
                hsv = h_s[:, dsc * 512:(dsc + 1) * 512]
                sch.ev("scalar", lambda e, hsv=hsv: e.activation(
                    hsv, pB[6][:], AF.Copy, scale=hn_sb[:, 0:1]),
                    waits=hs_cons, inc="act")
                hs_cons = []
                consumed("B6", "act")
                a_g = sch.cnt["act"]
                dd = sch.ev("vector", lambda e, hsv=hsv: e.scalar_tensor_tensor(
                    hsv, gs[:], 1.0, hsv, ALU.add, ALU.mult),
                    waits=[("act", a_g)], inc="dve")
                gs_cons2 = [("dve", sch.cnt["dve"])]
                dd = sch.ev("vector", lambda e, hsv=hsv: e.tensor_mul(hsv, hsv, pB[7][:]),
                            waits=[("pe", p_u), ("dve", dd)], inc="dve")
                consumed("B7", "dve")
                yield 16
            if t4 == 3:
                p_up = sch.cnt["pe"]
                WV_G0[0] = sch.dma(
                    "gpsimd", AR_W[:, 0:2048].bitcast(BF16).rearrange(
                        "p (c n) -> p c n", c=NCH),
                    AP(w_eg, 0, egpat0), waits=[("pe", p_up)], inc="wg0")
                WV_U0[0] = sch.dma(
                    "gpsimd", AR_W[:, 2048:4096].bitcast(BF16).rearrange(
                        "p (c n) -> p c n", c=NCH),
                    AP(w_eu, 0, egpat0), waits=[("pe", p_up)], inc="wu0")
                WV_D0[0] = sch.dma(
                    "gpsimd", AR_W[:, 4096:6144].bitcast(BF16).rearrange(
                        "p (c n) -> p c n", c=4),
                    AP(w_ed, 0, edpat0), waits=[("pe", p_up)], inc="wd0")
                WV_G0[1] = sch.dma(
                    "gpsimd", AR_W[:, 6144:8192].bitcast(BF16).rearrange(
                        "p (c n) -> p c n", c=NCH),
                    AP(w_eg, D * DEXP, egpat0), waits=[("pe", p_up)], inc="wg1")
            d_h = sch.cnt["dve"]
            for hc in range(NCH):
                pdi = 6 + hc % 2
                reg = f"B{pdi}"
                w0 = pe_w(reg) + [("dve", d_h)]
                for dsc in range(NCH):
                    sch.ev("tensor", lambda e, t4=t4, hc=hc, dsc=dsc, pdi=pdi: e.matmul(
                        pB[pdi][:], Wd_sh[:, dsc * 1024 + hc * 128: dsc * 1024 + (hc + 1) * 128],
                        h_s[:, dsc * 512:(dsc + 1) * 512],
                        start=(dsc == 0), stop=(dsc == NCH - 1)),
                        waits=(w0 if dsc == 0 else []), inc="pe")
                p_d = sch.cnt["pe"]
                roff = hc * L + t4 * 512
                ST["lrd"] = sch.ev("vector", lambda e, roff=roff, pdi=pdi: e.tensor_add(
                    resid[:, roff:roff + 512], resid[:, roff:roff + 512], pB[pdi][:]),
                    waits=[("pe", p_d), ("dve", ST["lrd"]),
                           ("act", ln_prog[2 * t4 + 1][1])], inc="dve")
                consumed(reg, "dve")
                yield 8
            hs_cons = [("pe", sch.cnt["pe"])]
        sh_done = sch.cnt["pe"]
        emit_cb(0)
        emit_cb(1)
        WV_U0[1] = sch.dma(
            "gpsimd", AR_W[:, 8192:10240].bitcast(BF16).rearrange(
                "p (c n) -> p c n", c=NCH),
            AP(w_eu, D * DEXP, egpat0), waits=[("pe", sh_done)], inc="wu1")
        WV_D0[1] = sch.dma(
            "gpsimd", AR_W[:, 10240:12288].bitcast(BF16).rearrange(
                "p (c n) -> p c n", c=4),
            AP(w_ed, D * DEXP, edpat0), waits=[("pe", sh_done)], inc="wd1")
        sch.dma("gpsimd", eb_all[:, 0:11 * 384].rearrange("p (h j) -> p h j", h=11),
                AP(expdram, 127, [[640, 128], [81920, 11], [1, 384]]),
                waits=[("ebD", 64), ("pe", sh_done)], inc="ebA")
        ST["gs_cons"] = list(gs_cons2)

    def expert_work():
        area = exp_area
        h2cons = [("ebD", 64)]
        gs_cons2 = list(ST.get("gs_cons", []))
        wv_g = list(WV_G0)
        wv_u = list(WV_U0)
        wv_d = list(WV_D0)
        wsem_g = ["wg0", "wg1"]
        wsem_u = ["wu0", "wu1"]
        wsem_d = ["wd0", "wd1"]
        egpat = [[DEXP, 128], [128 * DEXP, NCH], [1, DEXP]]
        edpat = [[D, 128], [128 * D, 4], [1, D]]
        for grp in range(4):
            if grp == 0:
                continue_loads = False
            else:
                continue_loads = True
            for si in range(2):
                if not continue_loads:
                    break
                ex = 2 * grp + si
                gv = AR_W[:, si * 6144:si * 6144 + 2048].bitcast(BF16)
                uv = AR_W[:, si * 6144 + 2048:si * 6144 + 4096].bitcast(BF16)
                dv = AR_W[:, si * 6144 + 4096:si * 6144 + 6144].bitcast(BF16)
                wq = [("pe", area[f"g{si}"])]
                wv_g[si] = sch.dma("gpsimd", gv.rearrange("p (c n) -> p c n", c=NCH),
                                   AP(w_eg, ex * D * DEXP, egpat), waits=wq, inc=wsem_g[si])
                wq = [("pe", area[f"u{si}"])]
                wv_u[si] = sch.dma("gpsimd", uv.rearrange("p (c n) -> p c n", c=NCH),
                                   AP(w_eu, ex * D * DEXP, egpat), waits=wq, inc=wsem_u[si])
                wq = [("pe", area["d"])]
                wv_d[si] = sch.dma("gpsimd", dv.rearrange("p (c n) -> p c n", c=4),
                                   AP(w_ed, ex * D * DEXP, edpat), waits=wq, inc=wsem_d[si])
            for t4 in range(4):
                for si in range(2):
                    kblk = (grp * 4 + t4) * 2 + si
                    scb = kblk % 2
                    cbv = cbs[scb]
                    gv = AR_W[:, si * 6144:si * 6144 + 2048].bitcast(BF16)
                    uv = AR_W[:, si * 6144 + 2048:si * 6144 + 4096].bitcast(BF16)
                    for dk in range(4):
                        w0 = pe_w("B6") + [(wsem_g[si], wv_g[si]), ("dve", ln_dve)]
                        for hc in range(NCH):
                            sch.ev("tensor", lambda e, t4=t4, dk=dk, hc=hc, gv=gv: e.matmul(
                                pB[6][:], gv[:, hc * 512 + dk * 128: hc * 512 + (dk + 1) * 128],
                                xn[:, hc * L + t4 * 512: hc * L + (t4 + 1) * 512],
                                start=(hc == 0), stop=(hc == NCH - 1)),
                                waits=(w0 if hc == 0 else []), inc="pe")
                        p_g = sch.cnt["pe"]
                        area[f"g{si}"] = p_g
                        w1 = pe_w("B7") + [(wsem_u[si], wv_u[si])]
                        for hc in range(NCH):
                            sch.ev("tensor", lambda e, t4=t4, dk=dk, hc=hc, uv=uv: e.matmul(
                                pB[7][:], uv[:, hc * 512 + dk * 128: hc * 512 + (dk + 1) * 128],
                                xn[:, hc * L + t4 * 512: hc * L + (t4 + 1) * 512],
                                start=(hc == 0), stop=(hc == NCH - 1)),
                                waits=(w1 if hc == 0 else []), inc="pe")
                        p_u = sch.cnt["pe"]
                        area[f"u{si}"] = p_u
                        sch.ev("scalar", lambda e: e.activation(gs[:], pB[6][:], AF.Tanh,
                                                                scale=hn_sb[:, 0:1]),
                               waits=[("pe", p_g), consts] + gs_cons2, inc="act")
                        hv = h2v[:, si * 2048 + dk * 512: si * 2048 + (dk + 1) * 512]
                        sch.ev("scalar", lambda e, hv=hv: e.activation(
                            hv, pB[6][:], AF.Copy, scale=hn_sb[:, 0:1]),
                            waits=h2cons, inc="act")
                        h2cons = []
                        consumed("B6", "act")
                        a_g = sch.cnt["act"]
                        dd = sch.ev("vector", lambda e, hv=hv: e.scalar_tensor_tensor(
                            hv, gs[:], 1.0, hv, ALU.add, ALU.mult),
                            waits=[("act", a_g)], inc="dve")
                        gs_cons2 = [("dve", sch.cnt["dve"])]
                        dd = sch.ev("vector", lambda e, hv=hv: e.tensor_mul(hv, hv, pB[7][:]),
                                    waits=[("pe", p_u), ("dve", dd)], inc="dve")
                        consumed("B7", "dve")
                        sch.ev("vector", lambda e, hv=hv, cbv=cbv: e.tensor_mul(
                            hv, hv, cbv[:]),
                            waits=[(f"cb{scb}", cb_v[scb]), ("dve", dd)], inc="dve")
                        yield 16
                    cb_cons[scb] = [("dve", sch.cnt["dve"])]
                    emit_cb(kblk + 2)
                d_h = sch.cnt["dve"]
                for hc in range(NCH):
                    pdi = 6 + hc % 2
                    reg = f"B{pdi}"
                    w0 = pe_w(reg) + [("dve", d_h),
                                      (wsem_d[0], wv_d[0]), (wsem_d[1], wv_d[1])]
                    first = True
                    for si in range(2):
                        dvv = AR_W[:, si * 6144 + 4096:si * 6144 + 6144].bitcast(BF16)
                        for dk in range(4):
                            sch.ev("tensor",
                                   lambda e, t4=t4, hc=hc, dk=dk, si=si, pdi=pdi, dvv=dvv:
                                   e.matmul(pB[pdi][:],
                                            dvv[:, dk * 1024 + hc * 128: dk * 1024 + (hc + 1) * 128],
                                            h2v[:, si * 2048 + dk * 512: si * 2048 + (dk + 1) * 512],
                                            start=(si == 0 and dk == 0),
                                            stop=(si == 1 and dk == 3)),
                                   waits=(w0 if first else []), inc="pe")
                            first = False
                    p_d = sch.cnt["pe"]
                    area["d"] = p_d
                    roff = hc * L + t4 * 512
                    ST["lrd"] = sch.ev("vector", lambda e, roff=roff, pdi=pdi: e.tensor_add(
                        resid[:, roff:roff + 512], resid[:, roff:roff + 512], pB[pdi][:]),
                        waits=[("pe", p_d), ("dve", ST["lrd"])], inc="dve")
                    consumed(reg, "dve")
                    yield 8
                h2cons = [("pe", sch.cnt["pe"])]

    def all_ffn():
        yield from shared_work()
        yield from expert_work()

    ffn_gen = all_ffn()
    ffn_alive = [True]
    SLOT = [0]
    FFN_MM = [0]
    TOT_MM = 768 + 3072

    def ffn_step():
        if ffn_alive[0]:
            try:
                FFN_MM[0] += next(ffn_gen)
            except StopIteration:
                ffn_alive[0] = False

    for g in range(8):
        ln_stats(g)
        if g >= 1:
            ln_finish(g - 1)
            gg = g - 1
            if gg >= 1:
                cap = min(64 * gg, 192 * ((gg + 1) // 2))
                while FFN_MM[0] < cap and ffn_alive[0]:
                    ffn_step()
        if g == 2:
            emit_p1_chain()
    ln_finish(7)
    ln_pe = sch.cnt["pe"]
    ln_act = sch.cnt["act"]
    ln_dve = sch.cnt["dve"]

    tap("xn", xn[:], [("dve", ln_dve)])
    tap("resid0", resid[:], [("act", ln_act)])
    if phases <= 0:
        return _finish(nc, sch)

    sch.ev("vector", lambda e: e.memset(Vaug[:, 64:2080:65], 1.0),
           waits=[("pe", ln_pe), ("act", ln_act)], inc="dve")

    PAIR_ORDER = [7, 6, 0, 1, 2, 3, 4, 5]
    wpair_load(PAIR_ORDER[0], 0, [])
    wpair_load(PAIR_ORDER[0], 1, [])
    eb_bc = h2t4[:, 0:2048]
    for q4e in range(4):
        sch.dma("gpsimd", eb_bc[:],
                AP(tdram, q4e * 4 * 512, [[0, 128], [1, 2048]]),
                waits=[("ebB", 16), ("ebD", 16 * q4e)], inc="ebC")
        sch.dma("gpsimd", AP(expdram, q4e * 4 * 81920, [[641, 128], [81920, 4], [1, 512]]),
                eb_bc.rearrange("p (h m) -> p h m", h=4),
                waits=[("ebC", 16 * (q4e + 1))], inc="ebD")
    sch.dma("gpsimd", eb_all[:, 11 * 384:16 * 384].rearrange("p (h j) -> p h j", h=5),
            AP(expdram, 127 + 11 * 81920, [[640, 128], [81920, 5], [1, 384]]),
            waits=[("ebD", 64)], inc="ebA")
    eb_readyA = ("ebA", sch.cnt["ebA"])
    eb_readyB = ("ebA", sch.cnt["ebA"] + 16)
    for ji, j in enumerate(PAIR_ORDER):
        pbanks = [2, 3, 6, 7] if ji == 0 else [0, 1, 2, 3]
        wpair_prep(j, [("pe", last_scores_pe)] if ji > 0 else [], pbanks)
        a_wp = sch.cnt["act"]
        for wi, (which, colbase, scale) in enumerate(
                (("q", 2048, 0.125), ("k", 0, None), ("v", 1024, None))):
            dstt = {"q": Qp, "k": Kp, "v": Vfm}[which]
            for t4 in range(4):
                bi = pbanks[t4 % len(pbanks)]
                reg = f"B{bi}"
                pdst = pB[bi]
                w0 = pe_w(reg) + [("act", a_wp), ("dve", ln_dve)]
                if which == "q":
                    w0 += [("pe", last_scores_pe)] if ji > 0 else []
                for dc in range(NCH):
                    sch.ev("tensor", lambda e, t4=t4, dc=dc, wi=wi, pdst=pdst: e.matmul(
                        pdst[:], w_pair[:, (wi * 8 + dc) * 128: (wi * 8 + dc + 1) * 128],
                        xn[:, dc * L + t4 * 512:dc * L + (t4 + 1) * 512],
                        start=(dc == 0), stop=(dc == NCH - 1)),
                        waits=(w0 if dc == 0 else []), inc="pe")
                pmm = sch.cnt["pe"]
                ew = [("pe", pmm)]
                if ji > 0:
                    ew.append(("pe", last_scores_pe))
                if scale is not None:
                    scap = hn_sb[:, 2:3]
                    sch.ev("scalar", lambda e, t4=t4, pdst=pdst, dstt=dstt, scap=scap:
                           e.activation(dstt[:, t4 * 512:(t4 + 1) * 512], pdst[:],
                                        AF.Copy, scale=scap[:, 0:1]),
                           waits=ew + [consts], inc="act")
                    consumed(reg, "act")
                else:
                    sch.ev("vector", lambda e, t4=t4, pdst=pdst, dstt=dstt:
                           e.tensor_copy(dstt[:, t4 * 512:(t4 + 1) * 512], pdst[:]),
                           waits=ew, inc="dve")
                    consumed(reg, "dve")
        a_pv = sch.cnt["act"]
        d_pv = sch.cnt["dve"]
        for tc in range(NT):
            bi = pbanks[tc % len(pbanks)]
            reg = f"B{bi}"
            pb = pBb[bi]
            sch.ev("tensor", lambda e, tc=tc, pb=pb: e.transpose(
                pb[:, 0:128], Vfm[:, tc * 128:(tc + 1) * 128], id_bf[:]),
                waits=pe_w(reg) + [("act", a_pv), ("dve", d_pv),
                                   ("pe", last_vaug_pe)], inc="pe")
            p_tr = sch.cnt["pe"]
            dst = Vaug[:, tc * 130: (tc + 1) * 130].rearrange(
                "p (g c) -> p g c", g=2)[:, :, 0:64]
            src = pb[:, 0:128].rearrange("p (g c) -> p g c", g=2)
            sch.ev("vector", lambda e, dst=dst, src=src: e.tensor_copy(dst, src),
                   waits=[("pe", p_tr)], inc="dve")
            consumed(reg, "dve")
        proj_evac_dve = sch.cnt["dve"]
        proj_evac_act = sch.cnt["act"]
        if ji == 0:
            router_exp()
        if ji < 7:
            wpair_load(PAIR_ORDER[ji + 1], 0, [("pe", sch.cnt["pe"])])
            wpair_load(PAIR_ORDER[ji + 1], 1, [("pe", sch.cnt["pe"])])

        for q4 in range(4):
            p_sc = {}
            a_exp_t = {}

            def tail_exp(kc):
                ping = kc % 2
                for hh in range(2):
                    sb = 2 * hh + ping
                    ab = 2 * hh + ping
                    sch.ev("scalar", lambda e, sb=sb, ab=ab: e.activation(
                        attb[ab][:], pB[sb][:], AF.Exp),
                        waits=[("pe", p_sc[(kc, hh)])] + attb_cons[ab], inc="act")
                    consumed(f"B{sb}", "act")
                    a_exp_t[(kc, hh)] = sch.cnt["act"]

            av_w = {}

            def tail_eb(kc):
                ping = kc % 2
                for hh in range(2):
                    ab = 2 * hh + ping
                    a_exp = a_exp_t[(kc, hh)]
                    h = 2 * j + hh
                    qt0 = q4 * 4
                    bt = [qt for qt in range(max(qt0, kc - 1), min(qt0 + 4, kc + 2))]
                    w_av = [("act", a_exp)]
                    d_eb = None
                    if bt:
                        c0 = (bt[0] - qt0) * 128
                        c1 = (bt[-1] - qt0) * 128 + 128
                        e0 = (bt[0] - kc + 1) * 128
                        e1 = (bt[-1] - kc + 1) * 128 + 128
                        ebr = eb_readyA if h >= 11 else eb_readyB
                        sch.ev("vector", lambda e, ab=ab, h=h, c0=c0, c1=c1, e0=e0, e1=e1:
                               e.tensor_mul(attb[ab][:, c0:c1], attb[ab][:, c0:c1],
                                            eb_all[:, h * 384 + e0:h * 384 + e1]),
                               waits=[("act", a_exp), ebr], inc="dve")
                        d_eb = sch.cnt["dve"]
                        w_av = [("dve", d_eb)]
                    av_w[(kc, hh)] = (w_av, d_eb)

            def tail(kc):
                ping = kc % 2
                for hh in range(2):
                    ab = 2 * hh + ping
                    w_av, d_eb = av_w[(kc, hh)]
                    reg = "B4" if hh == 0 else "B5"
                    w2 = list(w_av)
                    if kc == 0:
                        w2 += avdone[reg]
                    sch.ev("tensor", lambda e, hh=hh, kc=kc, ab=ab: e.matmul(
                        pB[4 + hh][0:65, :],
                        Vaug[:, kc * 130 + hh * 65: kc * 130 + (hh + 1) * 65],
                        attb[ab][:],
                        start=(kc == 0), stop=(kc == NT - 1)),
                        waits=w2, inc="pe")
                    attb_cons[ab] = [("pe", sch.cnt["pe"])]
                    if d_eb is not None:
                        attb_cons[ab].append(("dve", d_eb))

            for kc in range(NT):
                ping = kc % 2
                w_pair2 = (pe_w(f"B{0 + ping}") + pe_w(f"B{2 + ping}")
                           + [("dve", proj_evac_dve), ("act", proj_evac_act)])
                for hh in range(2):
                    sb = 2 * hh + ping
                    base = 64 * hh
                    sch.ev("tensor", lambda e, kc=kc, q4=q4, sb=sb, base=base: e.matmul(
                        pB[sb][:], Kp[base:base + 64, kc * 128:(kc + 1) * 128],
                        Qp[base:base + 64, q4 * 512:(q4 + 1) * 512],
                        start=True, stop=True, tile_position=(base, 0)),
                        waits=(w_pair2 if hh == 0 else []), inc="pe")
                    p_sc[(kc, hh)] = sch.cnt["pe"]
                if kc >= 1:
                    tail_exp(kc - 1)
                    tail_eb(kc - 1)
                SLOT[0] += 1
                tgt = (TOT_MM * SLOT[0]) // 512
                while FFN_MM[0] < tgt and ffn_alive[0]:
                    ffn_step()
                if kc >= 1:
                    tail(kc - 1)
                if ji == 0 and q4 == 0:
                    router_chain_step(kc)
                    coefT_step()
            tail_exp(NT - 1)
            tail_eb(NT - 1)
            tail(NT - 1)
            last_vaug_pe = sch.cnt["pe"]
            p_av = sch.cnt["pe"]
            newpend = []
            for hh in range(2):
                reg = "B4" if hh == 0 else "B5"
                slot = 2 * hh + (q4 % 2)
                sch.ev("vector", lambda e, hh=hh, slot=slot: e.tensor_copy(
                    av_ep[slot][:], pB[4 + hh][0:65, :]),
                    waits=[("pe", p_av)] + av_ep_cons[slot], inc="dve")
                avdone[reg] = [("dve", sch.cnt["dve"])]
                newpend.append({"hh": hh, "slot": slot, "q4": q4, "h": 2 * j + hh,
                                "evac": ("dve", sch.cnt["dve"])})
            epi_stage2()
            pend = newpend
            epi_stage1()
        last_scores_pe = sch.cnt["pe"]

    epi_stage2()
    pend = []
    while ffn_alive[0]:
        ffn_step()

    attn_dve = sch.cnt["dve"]
    attn_act = sch.cnt["act"]
    tap("resid_attn", resid[:], [("dve", attn_dve)])
    if phases <= 4:
        return _finish(nc, sch)

    tap("resid_sh", resid[:], [("dve", sch.cnt["dve"])])

    tap("resid_moe", resid[:], [("dve", sch.cnt["dve"])])
    if phases <= 7:
        return _finish(nc, sch)

    outst = [AR_M[:, 0:2048], AR_M[:, 2048:4096]]
    p7_dve = sch.cnt["dve"]
    out_cons = [[("pe", sch.cnt["pe"]), ("dve", p7_dve)],
                [("pe", sch.cnt["pe"]), ("dve", p7_dve)]]
    for g in range(8):
        sl = g % 2
        for ti in range(2):
            tc = 2 * g + ti
            for half in range(2):
                reg = ["B6", "B7"][(2 * ti + half) % 2]
                pb = pB[6 + (2 * ti + half) % 2][:]
                dcs = [half * 4 + i for i in range(4)]
                ws = pe_w(reg) + [("dve", p7_dve)]
                for i, dc in enumerate(dcs):
                    sch.ev("tensor", lambda e, i=i, dc=dc, tc=tc, pb=pb: e.transpose(
                        pb[:, i * 128:(i + 1) * 128],
                        resid[:, dc * L + tc * 128:dc * L + (tc + 1) * 128], id_f32[:]),
                        waits=(ws if i == 0 else []), inc="pe")
                p_t = sch.cnt["pe"]
                sch.ev("vector", lambda e, sl=sl, ti=ti, half=half, pb=pb: e.tensor_copy(
                    outst[sl][:, ti * 1024 + half * 512: ti * 1024 + (half + 1) * 512], pb[:]),
                    waits=[("pe", p_t)] + (out_cons[sl] if ti == 0 and half == 0 else []),
                    inc="dve")
                consumed(reg, "dve")
        d_o = sch.cnt["dve"]
        sch.dma("sync", AP(y_out, g * 256 * D, [[D, 128], [128 * D, 2], [1, D]]),
                outst[sl].rearrange("p (t c) -> p t c", t=2),
                waits=[("dve", d_o)], inc="outq")
        out_cons[sl] = [("outq", sch.cnt["outq"])]
    sch.ev("sync", lambda e: e.nop(), waits=[("outq", sch.cnt["outq"])])

    return _finish(nc, sch)


def _finish(nc, sch):
    with nc.allow_low_precision(reason="bf16 score writeout by design"):
        with nc.Block() as block:
            sch.emit(block)
    return nc


def _shard_inputs(inputs):
    x = np.ascontiguousarray(inputs["L0_attn_x"], dtype=np.float32)
    common = {
        "w_in": np.ascontiguousarray(inputs["L0_attn_w_in"], dtype=np.float32),
        "k_off": np.ascontiguousarray(inputs["L0_attn_kernel_offsets"], dtype=np.float32),
        "k_amp": np.ascontiguousarray(inputs["L0_attn_kernel_amplitudes"], dtype=np.float32),
        "k_sh": np.ascontiguousarray(inputs["L0_attn_kernel_sharpness"], dtype=np.float32),
        "w_rtr": np.ascontiguousarray(inputs["L0_ffn_router"], dtype=np.float32),
        "w_eg": np.ascontiguousarray(inputs["L0_ffn_experts_gate"], dtype=np.float32),
        "w_eu": np.ascontiguousarray(inputs["L0_ffn_experts_up"], dtype=np.float32),
        "w_ed": np.ascontiguousarray(inputs["L0_ffn_experts_down"], dtype=np.float32),
        "w_sg": np.ascontiguousarray(inputs["L0_ffn_shared_gate"], dtype=np.float32),
        "w_su": np.ascontiguousarray(inputs["L0_ffn_shared_up"], dtype=np.float32),
        "w_sd": np.ascontiguousarray(inputs["L0_ffn_shared_down"], dtype=np.float32),
    }
    return [dict(common, x=np.ascontiguousarray(x[b])) for b in range(B)]


def kernel(**inputs):
    nc = build_program()
    in_maps = _shard_inputs(inputs)
    res = run_bass_kernel_spmd(nc, in_maps, core_ids=list(range(B)))
    return np.stack([r["y"] for r in res.results], axis=0).astype(np.float32)


if __name__ == "__main__":
    nc = build_program(phases=0)
    print("build ok; instructions:", nc._state().n_instructions()
          if hasattr(nc, "_state") else "?")


# version 55 (final)
# speedup vs baseline: 1.5503x; 1.0046x over previous
import sys

sys.path.insert(0, "/opt/trn_rl_repo")

import numpy as np
import ml_dtypes

import concourse.bass as bass
import concourse.mybir as mybir
from concourse.ap import AP
from concourse.bass_utils import run_bass_kernel_spmd

F32 = mybir.dt.float32
BF16 = mybir.dt.bfloat16
AF = mybir.ActivationFunctionType
ALU = mybir.AluOpType
AX = mybir.AxisListType

B, L, D = 8, 2048, 1024
H = 16
NE, DEXP = 8, 512
NCH = 8
NT = 16
EPS = 1e-5


class Sched:

    ENGINES = ("sync", "gpsimd", "scalar", "vector", "tensor")

    def __init__(self, nc):
        self.nc = nc
        self.events = {e: [] for e in self.ENGINES}
        self.sems = {}
        self.cnt = {}

    def sem(self, name):
        if name not in self.sems:
            self.sems[name] = self.nc.alloc_semaphore(name)
            self.cnt[name] = 0
        return name

    def ev(self, eng, fn, waits=(), inc=None, amt=1):
        ws = []
        for s, v in waits:
            assert s in self.cnt, f"wait on unknown sem {s}"
            assert v <= self.cnt[s], f"wait {s}>={v} but only {self.cnt[s]} scheduled"
            if v > 0:
                ws.append((s, v))
        self.events[eng].append((fn, tuple(ws), inc, amt))
        if inc is not None:
            self.sem(inc)
            self.cnt[inc] += amt
            return self.cnt[inc]
        return None

    def dma(self, eng, out, in_, waits=(), inc=None):
        return self.ev(eng, lambda e: e.dma_start(out=out, in_=in_),
                       waits=waits, inc=inc, amt=16)

    def emit(self, block):
        def mkbody(name):
            evs = self.events[name]
            sems = self.sems

            def body(eng):
                guaranteed = {}
                for fn, waits, inc, amt in evs:
                    for s, v in waits:
                        if guaranteed.get(s, 0) < v:
                            eng.wait_ge(sems[s], v)
                            guaranteed[s] = v
                    ins = fn(eng)
                    if inc is not None:
                        ins.then_inc(sems[inc], amt)
            return body

        block.sync(mkbody("sync"))
        block.gpsimd(mkbody("gpsimd"))
        block.scalar(mkbody("scalar"))
        block.vector(mkbody("vector"))
        block.tensor(mkbody("tensor"))


def build_program(phases=99, taps=()):
    nc = bass.Bass("TRN2", target_bir_lowering=False)
    sch = Sched(nc)
    for s in ("pe", "act", "dve", "misc",
              "x0", "x1", "wst0", "wst1", "wst2", "wst3",
              "ebA", "ebB", "ebC", "ebD", "cbw",
              "wg0", "wu0", "wd0", "wg1", "wu1", "wd1",
              "cb0", "cb1", "rtrw", "shw", "outq"):
        sch.sem(s)

    x_in = nc.dram_tensor("x", [L, D], F32, kind="ExternalInput")
    w_in = nc.dram_tensor("w_in", [3 * D, D], F32, kind="ExternalInput")
    k_off = nc.dram_tensor("k_off", [H, 16], F32, kind="ExternalInput")
    k_amp = nc.dram_tensor("k_amp", [H, 16], F32, kind="ExternalInput")
    k_sh = nc.dram_tensor("k_sh", [H, 16], F32, kind="ExternalInput")
    w_rtr = nc.dram_tensor("w_rtr", [D, NE], F32, kind="ExternalInput")
    w_eg = nc.dram_tensor("w_eg", [NE, D, DEXP], F32, kind="ExternalInput")
    w_eu = nc.dram_tensor("w_eu", [NE, D, DEXP], F32, kind="ExternalInput")
    w_ed = nc.dram_tensor("w_ed", [NE, DEXP, D], F32, kind="ExternalInput")
    w_sg = nc.dram_tensor("w_sg", [D, D], F32, kind="ExternalInput")
    w_su = nc.dram_tensor("w_su", [D, D], F32, kind="ExternalInput")
    w_sd = nc.dram_tensor("w_sd", [D, D], F32, kind="ExternalInput")
    y_out = nc.dram_tensor("y", [L, D], F32, kind="ExternalOutput")

    tdram = nc.dram_tensor("tdram", [16 * 512], F32, kind="Internal")
    expdram = nc.dram_tensor("expdram", [16 * 128 * 640 + 640], F32, kind="Internal")
    coefdram = nc.dram_tensor("coefdram", [NE * L], F32, kind="Internal")

    ident_bf = nc.inline_tensor(np.eye(128, dtype=ml_dtypes.bfloat16), "ident_bf")
    ident_f32 = nc.inline_tensor(np.eye(128, dtype=np.float32), "ident_f32")
    ones64 = nc.inline_tensor(np.ones((1, 64), dtype=ml_dtypes.bfloat16), "ones64c")
    hn_c = nc.inline_tensor(np.concatenate(
        [np.full((128, 1), 0.5, dtype=np.float32),
         np.full((128, 1), -1.0, dtype=np.float32),
         np.full((128, 1), 0.125, dtype=np.float32),
         np.full((128, 1), 1.0, dtype=np.float32)], axis=1), "hn_c")
    drow = nc.inline_tensor((np.arange(512, dtype=np.float32) - 255.0)[None, :], "drow")
    sel = np.zeros((2, 128, 16), dtype=np.float32)
    for hb in range(2):
        for p in range(128):
            sel[hb, p, hb * 8 + p // 16] = 1.0
    sel_c = nc.inline_tensor(np.ascontiguousarray(sel.reshape(2 * 128, 16)), "sel_c")

    resid = nc.alloc_sbuf_tensor("resid", [128, NCH * L], F32)
    xn = nc.alloc_sbuf_tensor("xn", [128, NCH * L], BF16)
    id_bf = nc.alloc_sbuf_tensor("id_bf", [128, 128], BF16)
    id_f32 = nc.alloc_sbuf_tensor("id_f32", [128, 128], F32)
    ones_sb = nc.alloc_sbuf_tensor("ones_sb", [128, 64], BF16)
    sel_sb = nc.alloc_sbuf_tensor("sel_sb", [128, 32], F32)
    rtr_sb = nc.alloc_sbuf_tensor("rtr_sb", [128, NCH * NE], BF16)
    hn_sb = nc.alloc_sbuf_tensor("hn_sb", [128, 4], F32)
    AR_W = nc.alloc_sbuf_tensor("AR_W", [128, 12288], F32)
    AR_M = nc.alloc_sbuf_tensor("AR_M", [128, 7952], F32)
    AR_E = nc.alloc_sbuf_tensor("AR_E", [128, 3072], F32)
    w_pair = nc.alloc_sbuf_tensor("w_pair", [128, 3072], BF16)
    wstage = nc.alloc_sbuf_tensor("wstage", [128, 2048], BF16)
    h2t4 = nc.alloc_sbuf_tensor("h2t4", [128, 2048], F32)

    pB = [nc.alloc_psum_tensor(f"pb{i}", [128, 512], F32) for i in range(8)]
    pBb = [pB[i][:, 0:512].bitcast(BF16) for i in range(8)]

    regw = {f"B{i}": [] for i in range(8)}

    def pe_w(*regions):
        ws = []
        for r in regions:
            ws += regw[r]
        return ws

    def consumed(region, sem):
        regw[region] = [(sem, sch.cnt[sem])]

    tapdefs = []
    ST = {"lrd": 0}

    def tap(name, ap, waits):
        if name in taps:
            th = nc.dram_tensor("tap_" + name, list(ap.shape), ap.dtype,
                                kind="ExternalOutput")
            sch.dma("sync", th[:], ap, waits=waits, inc="misc")

    sch.dma("sync", id_bf[:], ident_bf[:], inc="misc")
    sch.dma("sync", id_f32[:], ident_f32[:], inc="misc")
    sch.dma("sync", ones_sb[:], AP(ones64, 0, [[0, 128], [1, 64]]), inc="misc")
    sch.dma("sync", sel_sb[:, 0:16], AP(sel_c, 0, [[16, 128], [1, 16]]), inc="misc")
    sch.dma("sync", sel_sb[:, 16:32], AP(sel_c, 128 * 16, [[16, 128], [1, 16]]), inc="misc")
    sch.dma("sync", hn_sb[:], hn_c[:], inc="misc")
    consts = ("misc", sch.cnt["misc"])

    tb_cols = AR_M[:, 6160:6176]
    tb_selamp = AR_M[:, 6176:6192]
    tb_dbc = AR_M[:, 6192:6704]
    tb_tmp = h2t4[:, 0:512]
    tb_ebt = AR_M[:, 7216:7728]
    eb_all = AR_E[:, 0:3072].bitcast(BF16)

    sch.dma("scalar", tb_dbc[:], AP(drow, 0, [[0, 128], [1, 512]]), inc="ebA")
    for hb in range(2):
        base = hb * 128
        sch.dma("scalar", tb_cols[:, 0 + hb:1 + hb], AP(k_off, base, [[1, 128], [1, 1]]), inc="ebA")
        sch.dma("scalar", tb_cols[:, 2 + hb:3 + hb], AP(k_amp, base, [[1, 128], [1, 1]]), inc="ebA")
        sch.dma("scalar", tb_cols[:, 4 + hb:5 + hb], AP(k_sh, base, [[1, 128], [1, 1]]), inc="ebA")
    ebl = ("ebA", sch.cnt["ebA"])

    tb_tmp2 = h2t4[:, 512:1024]

    def emit_p1_chain():
      prev_half = []
      for hb in range(2):
        oc = tb_cols[:, 0 + hb:1 + hb]
        ac = tb_cols[:, 2 + hb:3 + hb]
        sc = tb_cols[:, 4 + hb:5 + hb]
        d_sub = sch.ev("vector", lambda e, oc=oc: e.tensor_scalar(
            tb_tmp[:], tb_dbc[:], oc, None, ALU.subtract),
            waits=[ebl] + prev_half, inc="dve")
        d_sq = sch.ev("vector", lambda e: e.tensor_mul(tb_tmp[:], tb_tmp[:], tb_tmp[:]),
                      waits=[("dve", d_sub)], inc="dve")
        d_t = sch.ev("vector", lambda e, sc=sc: e.tensor_scalar(
            tb_tmp[:], tb_tmp[:], sc, None, ALU.mult), waits=[("dve", d_sq)], inc="dve")
        d_n = sch.ev("vector", lambda e: e.tensor_scalar(
            tb_tmp2[:], tb_tmp[:], -1.0, None, ALU.mult), waits=[("dve", d_t)], inc="dve")
        d3 = sch.ev("vector", lambda e: e.tensor_tensor(
            tb_tmp[:], tb_tmp[:], tb_tmp2[:], ALU.min), waits=[("dve", d_n)], inc="dve")
        sch.ev("scalar", lambda e: e.activation(tb_tmp[:], tb_tmp[:], AF.Exp),
               waits=[("dve", d3)], inc="act")
        a3 = sch.cnt["act"]
        sch.ev("vector", lambda e, hb=hb, ac=ac: e.tensor_scalar(
            tb_selamp[:], sel_sb[:, hb * 16:(hb + 1) * 16], ac, None, ALU.mult),
            waits=[consts], inc="dve")
        d4 = sch.cnt["dve"]
        sch.ev("tensor", lambda e, hb=hb: e.matmul(pB[1][0:16, :], tb_selamp[:], tb_tmp[:],
                                                   start=(hb == 0), stop=(hb == 1)),
               waits=pe_w("B1") + [("act", a3), ("dve", d4)], inc="pe")
        prev_half = [("act", a3), ("pe", sch.cnt["pe"])]
      p_T = sch.cnt["pe"]
      tap("tb_tmp", tb_tmp[:], [("act", sch.cnt["act"])])
      tap("tb_selamp", tb_selamp[:], [("dve", sch.cnt["dve"])])
      tap("tb_cols", tb_cols[:], [("dve", sch.cnt["dve"])])
      sch.ev("scalar", lambda e: e.activation(tb_ebt[0:16, :], pB[1][0:16, :], AF.Exp),
           waits=[("pe", p_T)], inc="act")
      tap("tb_ebt", tb_ebt[:], [("act", sch.cnt["act"])])
      consumed("B1", "act")
      a_ebt = sch.cnt["act"]
      sch.dma("scalar", AP(tdram, 0, [[512, 16], [1, 512]]), tb_ebt[0:16, :],
            waits=[("act", a_ebt)], inc="ebB")


    if phases <= 1:
        return _finish(nc, sch)

    coef_all = AR_M[:, 5632:5760]
    coef_scr = AR_M[:, 5760:5792]
    e8_all = AR_M[:, 5120:5248]
    coefT_sb = h2t4[0:8, 0:2048]
    coef_wr = None
    rtab = {"d_c": [None] * NT, "done": 0, "p_all": 0}

    sch.dma("gpsimd", rtr_sb.rearrange("p (c e) -> p c e", c=NCH),
            AP(w_rtr, 0, [[NE, 128], [128 * NE, NCH], [1, NE]]),
            inc="rtrw")
    rtr_rd = ("rtrw", sch.cnt["rtrw"])

    def router_exp():
        sch.ev("scalar", lambda e: e.activation(e8_all[:], pB[0][:, 0:128], AF.Exp),
               waits=[("pe", rtab["p_all"])], inc="act")
        consumed("B0", "act")
        rtab["a_e"] = sch.cnt["act"]

    def router_chain_step(tc):
        st4 = coef_scr[:, 8:12]
        probs = coef_scr[:, 12:20]
        mask = coef_scr[:, 20:28]
        if True:
            e8 = e8_all[:, tc * 8:(tc + 1) * 8]
            dd = sch.ev("vector", lambda e, e8=e8: e.reduce_sum(
                st4[:, 0:1], e8, axis=AX.X),
                waits=([("act", rtab["a_e"])] if tc == 0 else []), inc="dve")
            dd = sch.ev("vector", lambda e: e.reciprocal(st4[:, 1:2], st4[:, 0:1]),
                        waits=[("dve", dd)], inc="dve")
            dd = sch.ev("vector", lambda e, e8=e8: e.tensor_scalar(
                probs, e8, st4[:, 1:2], None, ALU.mult),
                waits=[("dve", dd)], inc="dve")
            dd = sch.ev("vector", lambda e: e.reduce_max(st4[:, 2:3], probs, axis=AX.X),
                        waits=[("dve", dd)], inc="dve")
            dd = sch.ev("vector", lambda e: e.tensor_scalar(
                mask, probs, st4[:, 2:3], -1e30, ALU.is_equal, ALU.mult),
                waits=[("dve", dd)], inc="dve")
            dd = sch.ev("vector", lambda e: e.tensor_add(mask, mask, probs),
                        waits=[("dve", dd)], inc="dve")
            dd = sch.ev("vector", lambda e: e.reduce_max(st4[:, 3:4], mask, axis=AX.X),
                        waits=[("dve", dd)], inc="dve")
            dd = sch.ev("vector", lambda e: e.tensor_scalar(
                mask, probs, st4[:, 3:4], None, ALU.is_ge),
                waits=[("dve", dd)], inc="dve")
            dd = sch.ev("vector", lambda e, tc=tc: e.tensor_mul(
                coef_all[:, tc * 8:(tc + 1) * 8], mask, probs),
                waits=[("dve", dd)], inc="dve")
            rtab["d_c"][tc] = sch.cnt["dve"]
            if tc == NT - 1:
                av_ep_cons[0] = [("dve", sch.cnt["dve"])]

    def coefT_step():
        tc = rtab["done"]
        if tc >= NT:
            return
        rtab["done"] += 1
        sch.ev("tensor", lambda e, tc=tc: e.transpose(
            pB[7][0:8, 0:128], coef_all[:, tc * 8:(tc + 1) * 8], id_f32[:]),
            waits=pe_w("B7") + [("dve", rtab["d_c"][tc])], inc="pe")
        p_t = sch.cnt["pe"]
        sch.ev("vector", lambda e, tc=tc: e.tensor_copy(
            coefT_sb[:, tc * 128:(tc + 1) * 128], pB[7][0:8, 0:128]),
            waits=[("pe", p_t)] + ([("ebD", 64)] if tc == 0 else []), inc="dve")
        consumed("B7", "dve")
        if tc == NT - 1:
            nonlocal coef_wr
            d_ct = sch.cnt["dve"]
            sch.dma("scalar", AP(coefdram, 0, [[2048, 8], [1, 2048]]), coefT_sb[:],
                    waits=[("dve", d_ct)], inc="cbw")
            coef_wr = ("cbw", sch.cnt["cbw"])
            av_ep_cons[1] = [("pe", p_t)]
            av_ep_cons[2] = [("pe", p_t)]


    stage = [AR_M[:, 0:2048], AR_M[:, 2048:4096]]
    xn_tm = [[AR_M[:, 4096:4608].bitcast(BF16), AR_M[:, 4608:5120].bitcast(BF16)],
             [AR_M[:, 5120:5632].bitcast(BF16), AR_M[:, 5632:6144].bitcast(BF16)]]
    sq_scr_p = [AR_M[:, 4096:5120], AR_M[:, 5120:6144]]
    lncol = AR_M[:, 6144:6160]

    xsem = ["x0", "x1"]
    xfree = [[], []]
    ln_prog = []
    xv_p = {}
    a_xn_p = {}

    def ln_stats(g):
        sl = g % 2
        xv = sch.dma("sync" if sl == 0 else "scalar",
                     stage[sl].rearrange("p (t c) -> p t c", t=2),
                     AP(x_in, g * 256 * D, [[D, 128], [128 * D, 2], [1, D]]),
                     waits=xfree[sl], inc=xsem[sl])
        xv_p[g] = xv
        a0l = 0
        for ti in range(2):
            xt = stage[sl][:, ti * 1024:(ti + 1) * 1024]
            sch.ev("vector", lambda e, xt=xt, ti=ti: e.reduce_sum(
                lncol[:, ti:ti + 1], xt, axis=AX.X), waits=[(xsem[sl], xv)], inc="dve")
            a0l = sch.ev("scalar", lambda e, xt=xt, ti=ti, sl=sl: e.activation(
                sq_scr_p[sl][:], xt, AF.Square, accum_out=lncol[:, 2 + ti:3 + ti]),
                waits=[(xsem[sl], xv)], inc="act")
        d_s = sch.cnt["dve"]
        d_nm = sch.ev("vector", lambda e: e.tensor_scalar(
            lncol[:, 4:6], lncol[:, 0:2], -1.0 / D, None, ALU.mult),
            waits=[("dve", d_s)], inc="dve")
        d_m2 = sch.ev("vector", lambda e: e.tensor_scalar(
            lncol[:, 6:8], lncol[:, 2:4], 1.0 / D, None, ALU.mult),
            waits=[("act", a0l)], inc="dve")
        d_nq = sch.ev("vector", lambda e: e.scalar_tensor_tensor(
            lncol[:, 8:10], lncol[:, 4:6], -1.0, lncol[:, 4:6],
            ALU.mult, ALU.mult), waits=[("dve", d_nm)], inc="dve")
        d1 = sch.ev("vector", lambda e: e.scalar_tensor_tensor(
            lncol[:, 8:10], lncol[:, 8:10], EPS, lncol[:, 6:8],
            ALU.add, ALU.add), waits=[("dve", d_nq), ("dve", d_m2)], inc="dve")
        a1h = sch.ev("scalar", lambda e: e.activation(
            lncol[:, 10:12], lncol[:, 8:10], AF.Ln),
            waits=[("dve", d1)], inc="act")
        d1h = sch.ev("vector", lambda e: e.tensor_scalar(
            lncol[:, 10:12], lncol[:, 10:12], -0.5, None, ALU.mult),
            waits=[("act", a1h)], inc="dve")
        a1 = sch.ev("scalar", lambda e: e.activation(
            lncol[:, 12:14], lncol[:, 10:12], AF.Exp),
            waits=[("dve", d1h)], inc="act")
        a_xn = []
        for ti in range(2):
            xt = stage[sl][:, ti * 1024:(ti + 1) * 1024]
            sch.ev("vector", lambda e, xt=xt, sl=sl, ti=ti: e.tensor_scalar(
                xn_tm[sl][ti][:], xt, lncol[:, 4 + ti:5 + ti],
                lncol[:, 12 + ti:13 + ti], ALU.add, ALU.mult),
                waits=[("act", a1), ("dve", d_nm)], inc="dve")
            a_xn.append(("dve", sch.cnt["dve"]))
        a_xn_p[g] = a_xn

    def ln_finish(g):
        sl = g % 2
        xv = xv_p[g]
        a_xn = a_xn_p[g]
        for dc in range(NCH):
            rb = ["B6", "B7"][dc % 2]
            rf = ["B4", "B5"][dc % 2]
            pb = pBb[6 + dc % 2][:, 0:256]
            pf = pB[4 + dc % 2][:, 0:256]
            for ti in range(2):
                sch.ev("tensor", lambda e, ti=ti, dc=dc, pb=pb, sl=sl: e.transpose(
                    pb[:, ti * 128:(ti + 1) * 128],
                    xn_tm[sl][ti][:, dc * 128:(dc + 1) * 128], id_bf[:]),
                    waits=pe_w(rb) + [a_xn[ti], consts], inc="pe")
                sch.ev("tensor", lambda e, ti=ti, dc=dc, pf=pf, sl=sl: e.transpose(
                    pf[:, ti * 128:(ti + 1) * 128],
                    stage[sl][:, ti * 1024 + dc * 128: ti * 1024 + (dc + 1) * 128],
                    id_f32[:]),
                    waits=pe_w(rf) + [(xsem[sl], xv)], inc="pe")
            p_tr = sch.cnt["pe"]
            sch.ev("vector", lambda e, dc=dc, g=g, pb=pb: e.tensor_copy(
                xn[:, dc * L + g * 256: dc * L + g * 256 + 256], pb[:]),
                waits=[("pe", p_tr)], inc="dve")
            consumed(rb, "dve")
            sch.ev("scalar", lambda e, dc=dc, g=g, pf=pf: e.copy(
                resid[:, dc * L + g * 256: dc * L + g * 256 + 256], pf[:]),
                waits=[("pe", p_tr)], inc="act")
            consumed(rf, "act")
        xfree[sl] = [("pe", sch.cnt["pe"]), a_xn[1]]
        ln_prog.append((sch.cnt["dve"], sch.cnt["act"]))
        for tc2 in (2 * g, 2 * g + 1):
            w0r = (pe_w("B0") + [rtr_rd] if tc2 == 0 else []) + \
                  [("dve", ln_prog[g][0])]
            for dc in range(NCH):
                sch.ev("tensor", lambda e, tc2=tc2, dc=dc: e.matmul(
                    pB[0][:, tc2 * 8:(tc2 + 1) * 8],
                    xn[:, dc * L + tc2 * 128:dc * L + (tc2 + 1) * 128],
                    rtr_sb[:, dc * 8:(dc + 1) * 8],
                    start=(dc == 0), stop=(dc == NCH - 1)),
                    waits=(w0r if dc == 0 else []), inc="pe")
        if g == 7:
            rtab["p_all"] = sch.cnt["pe"]

    Wg_sh = AR_W[:, 0:4096].bitcast(BF16)
    Wu_sh = AR_W[:, 4096:8192].bitcast(BF16)
    Wd_sh = AR_W[:, 8192:12288].bitcast(BF16)
    h_s = AR_E[:, 0:2048].bitcast(BF16)
    gs = AR_M[:, 6928:7184].bitcast(BF16)

    wpat = [[D, 128], [128 * D, NCH], [1, D]]
    sch.dma("gpsimd", Wg_sh.rearrange("p (c n) -> p c n", c=NCH), AP(w_sg, 0, wpat),
            inc="shw")
    sch.dma("gpsimd", Wu_sh.rearrange("p (c n) -> p c n", c=NCH), AP(w_su, 0, wpat),
            inc="shw")
    sch.dma("gpsimd", Wd_sh.rearrange("p (c n) -> p c n", c=NCH), AP(w_sd, 0, wpat),
            inc="shw")
    sh_rd = ("shw", sch.cnt["shw"])


    wsem = ["wst0", "wst1", "wst2"]
    wst_v = [0, 0, 0]
    wst_slot = {0: 0, 1: 1, 2: 0}
    ROWB = {0: 2048, 1: 0, 2: 1024}

    def wpair_load(j, which, waits):
        sl = wst_slot[which]
        wst_v[which] = sch.dma(
            "gpsimd", wstage[:, sl * 1024:(sl + 1) * 1024],
            AP(w_in, (ROWB[which] + j * 128) * D, [[D, 128], [1, D]]),
            waits=waits, inc=wsem[which])

    def wpair_prep(j, extra, banks):
        for which in range(3):
            sl = wst_slot[which]
            if which == 2:
                wpair_load(j, 2, [("pe", sch.cnt["pe"])])
            for g2 in range(2):
                bi = banks[(2 * which + g2) % len(banks)]
                reg = f"B{bi}"
                pb = pBb[bi][:, 0:512]
                for i in range(4):
                    dc = g2 * 4 + i
                    sch.ev("tensor", lambda e, which=which, dc=dc, i=i, pb=pb, sl=sl: e.transpose(
                        pb[:, i * 128:(i + 1) * 128],
                        wstage[:, sl * 1024 + dc * 128: sl * 1024 + (dc + 1) * 128],
                        id_bf[:]),
                        waits=(pe_w(reg) + [(wsem[which], wst_v[which]), consts] + extra
                               if i == 0 else []), inc="pe")
                p_tr = sch.cnt["pe"]
                sch.ev("vector", lambda e, which=which, g2=g2, pb=pb: e.tensor_copy(
                    w_pair[:, (which * 8 + g2 * 4) * 128: (which * 8 + g2 * 4 + 4) * 128],
                    pb[:]),
                    waits=[("pe", p_tr)], inc="dve")
                consumed(reg, "dve")

    if phases <= 2:
        return _finish(nc, sch)

    Qp = AR_M[:, 0:1024].bitcast(BF16)
    Kp = AR_M[:, 1024:2048].bitcast(BF16)
    Vaug = AR_M[:, 2048:3088].bitcast(BF16)
    Vfm = AR_M[:, 3088:4112].bitcast(BF16)
    attb = [AR_M[:, 4112 + i * 256: 4368 + i * 256].bitcast(BF16) for i in range(4)]
    rb_sb = AR_M[0:64, 6672:6928].bitcast(BF16)
    tmp_ep = AR_M[:, 7184:7440].bitcast(BF16)
    rrow = [AR_M[0:1, 7440:7696].bitcast(BF16),
            AR_M[0:1, 7696:7952].bitcast(BF16)]
    av_ep = [AR_M[0:65, 5136 + i * 256: 5392 + i * 256].bitcast(BF16)
             for i in range(4)]

    last_scores_pe = 0
    last_vaug_pe = 0
    proj_evac_dve = 0
    attb_cons = [[] for _ in range(4)]
    avdone = {"B4": list(regw["B4"]), "B5": list(regw["B5"])}
    av_ep_cons = [[], [], [], []]
    rrow_cons = [[], []]
    rb_sb_cons = []
    pend = []

    def epi_stage1():
        for P in pend:
            sch.ev("scalar", lambda e, P=P: e.activation(
                rrow[P["hh"]][:], av_ep[P["slot"]][64:65, :], AF.Ln),
                waits=[P["evac"], consts] + rrow_cons[P["hh"]], inc="act")
            P["d_r"] = sch.ev("scalar", lambda e, P=P: e.activation(
                rrow[P["hh"]][:], rrow[P["hh"]][:], AF.Exp,
                scale=hn_sb[0:1, 1:2]),
                waits=[("act", sch.cnt["act"])], inc="act")

    def epi_stage2():
        for P in pend:
            hh, q4p, h = P["hh"], P["q4"], P["h"]
            sch.ev("tensor", lambda e, P=P: e.matmul(
                pB[7][0:64, :], ones_sb[0:1, :], rrow[P["hh"]][:],
                start=True, stop=True),
                waits=pe_w("B7") + [("act", P["d_r"])], inc="pe")
            rrow_cons[hh] = [("pe", sch.cnt["pe"])]
            p_rb = sch.cnt["pe"]
            sch.ev("scalar", lambda e: e.copy(rb_sb[:], pB[7][0:64, :]),
                   waits=[("pe", p_rb)] + rb_sb_cons, inc="act")
            consumed("B7", "act")
            a_rb = sch.cnt["act"]
            rbase = (h % 2) * 64
            d_m = sch.ev("vector", lambda e, P=P, rbase=rbase: e.tensor_mul(
                tmp_ep[rbase:rbase + 64, :], av_ep[P["slot"]][0:64, :], rb_sb[:]),
                waits=[("act", a_rb)], inc="dve")
            rb_sb_cons[:] = [("dve", d_m)]
            roff = (h // 2) * L + q4p * 512
            ST["lrd"] = sch.ev(
                "vector", lambda e, rbase=rbase, roff=roff: e.tensor_add(
                    resid[rbase:rbase + 64, roff:roff + 512],
                    resid[rbase:rbase + 64, roff:roff + 512],
                    tmp_ep[rbase:rbase + 64, :]),
                waits=[("dve", d_m), ("dve", ST["lrd"])], inc="dve")
            av_ep_cons[P["slot"]] = [("dve", sch.cnt["dve"])]


    cbs = [AR_M[:, 6160:6416].bitcast(BF16),
           AR_M[:, 6416:6672].bitcast(BF16)]
    gs = AR_M[:, 6928:7184].bitcast(BF16)
    h2v = h2t4[:, 0:2048].bitcast(BF16)

    exp_area = {}
    WV_G0 = [0, 0]
    WV_U0 = [0, 0]
    WV_D0 = [0, 0]
    cb_v = [0, 0]
    cb_cons = [[], []]
    cb_seq = [(g, t, s2) for g in range(4) for t in range(4) for s2 in range(2)]
    egpat0 = [[DEXP, 128], [128 * DEXP, NCH], [1, DEXP]]
    edpat0 = [[D, 128], [128 * D, 4], [1, D]]

    def emit_cb(k):
        if k >= len(cb_seq):
            return
        g, t, s2 = cb_seq[k]
        ex = 2 * g + s2
        sl = k % 2
        assert coef_wr is not None
        cb_v[sl] = sch.dma("gpsimd", cbs[sl][:],
                           AP(coefdram, ex * L + t * 512, [[0, 128], [1, 512]]),
                           waits=[coef_wr] + cb_cons[sl], inc=f"cb{sl}")

    def shared_work():
        gs_cons2 = []
        hs_cons = []
        for t4 in range(4):
            for dsc in range(NCH):
                w0 = pe_w("B6") + [sh_rd, ("dve", ln_prog[2 * t4 + 1][0])]
                for hc in range(NCH):
                    sch.ev("tensor", lambda e, t4=t4, dsc=dsc, hc=hc: e.matmul(
                        pB[6][:], Wg_sh[:, hc * 1024 + dsc * 128: hc * 1024 + (dsc + 1) * 128],
                        xn[:, hc * L + t4 * 512: hc * L + (t4 + 1) * 512],
                        start=(hc == 0), stop=(hc == NCH - 1)),
                        waits=(w0 if hc == 0 else []), inc="pe")
                p_g = sch.cnt["pe"]
                w1 = pe_w("B7")
                for hc in range(NCH):
                    sch.ev("tensor", lambda e, t4=t4, dsc=dsc, hc=hc: e.matmul(
                        pB[7][:], Wu_sh[:, hc * 1024 + dsc * 128: hc * 1024 + (dsc + 1) * 128],
                        xn[:, hc * L + t4 * 512: hc * L + (t4 + 1) * 512],
                        start=(hc == 0), stop=(hc == NCH - 1)),
                        waits=(w1 if hc == 0 else []), inc="pe")
                p_u = sch.cnt["pe"]
                sch.ev("scalar", lambda e: e.activation(gs[:], pB[6][:], AF.Tanh,
                                                        scale=hn_sb[:, 0:1]),
                       waits=[("pe", p_g), consts] + gs_cons2, inc="act")
                hsv = h_s[:, dsc * 512:(dsc + 1) * 512]
                sch.ev("scalar", lambda e, hsv=hsv: e.activation(
                    hsv, pB[6][:], AF.Copy, scale=hn_sb[:, 0:1]),
                    waits=hs_cons, inc="act")
                hs_cons = []
                consumed("B6", "act")
                a_g = sch.cnt["act"]
                dd = sch.ev("vector", lambda e, hsv=hsv: e.scalar_tensor_tensor(
                    hsv, gs[:], 1.0, hsv, ALU.add, ALU.mult),
                    waits=[("act", a_g)], inc="dve")
                gs_cons2 = [("dve", sch.cnt["dve"])]
                dd = sch.ev("vector", lambda e, hsv=hsv: e.tensor_mul(hsv, hsv, pB[7][:]),
                            waits=[("pe", p_u), ("dve", dd)], inc="dve")
                consumed("B7", "dve")
                yield 16
            if t4 == 3:
                p_up = sch.cnt["pe"]
                WV_G0[0] = sch.dma(
                    "gpsimd", AR_W[:, 0:2048].bitcast(BF16).rearrange(
                        "p (c n) -> p c n", c=NCH),
                    AP(w_eg, 0, egpat0), waits=[("pe", p_up)], inc="wg0")
                WV_U0[0] = sch.dma(
                    "gpsimd", AR_W[:, 2048:4096].bitcast(BF16).rearrange(
                        "p (c n) -> p c n", c=NCH),
                    AP(w_eu, 0, egpat0), waits=[("pe", p_up)], inc="wu0")
                WV_D0[0] = sch.dma(
                    "gpsimd", AR_W[:, 4096:6144].bitcast(BF16).rearrange(
                        "p (c n) -> p c n", c=4),
                    AP(w_ed, 0, edpat0), waits=[("pe", p_up)], inc="wd0")
                WV_G0[1] = sch.dma(
                    "gpsimd", AR_W[:, 6144:8192].bitcast(BF16).rearrange(
                        "p (c n) -> p c n", c=NCH),
                    AP(w_eg, D * DEXP, egpat0), waits=[("pe", p_up)], inc="wg1")
            d_h = sch.cnt["dve"]
            for hc in range(NCH):
                pdi = 6 + hc % 2
                reg = f"B{pdi}"
                w0 = pe_w(reg) + [("dve", d_h)]
                for dsc in range(NCH):
                    sch.ev("tensor", lambda e, t4=t4, hc=hc, dsc=dsc, pdi=pdi: e.matmul(
                        pB[pdi][:], Wd_sh[:, dsc * 1024 + hc * 128: dsc * 1024 + (hc + 1) * 128],
                        h_s[:, dsc * 512:(dsc + 1) * 512],
                        start=(dsc == 0), stop=(dsc == NCH - 1)),
                        waits=(w0 if dsc == 0 else []), inc="pe")
                p_d = sch.cnt["pe"]
                roff = hc * L + t4 * 512
                ST["lrd"] = sch.ev("vector", lambda e, roff=roff, pdi=pdi: e.tensor_add(
                    resid[:, roff:roff + 512], resid[:, roff:roff + 512], pB[pdi][:]),
                    waits=[("pe", p_d), ("dve", ST["lrd"]),
                           ("act", ln_prog[2 * t4 + 1][1])], inc="dve")
                consumed(reg, "dve")
                yield 8
            hs_cons = [("pe", sch.cnt["pe"])]
        sh_done = sch.cnt["pe"]
        emit_cb(0)
        emit_cb(1)
        WV_U0[1] = sch.dma(
            "gpsimd", AR_W[:, 8192:10240].bitcast(BF16).rearrange(
                "p (c n) -> p c n", c=NCH),
            AP(w_eu, D * DEXP, egpat0), waits=[("pe", sh_done)], inc="wu1")
        WV_D0[1] = sch.dma(
            "gpsimd", AR_W[:, 10240:12288].bitcast(BF16).rearrange(
                "p (c n) -> p c n", c=4),
            AP(w_ed, D * DEXP, edpat0), waits=[("pe", sh_done)], inc="wd1")
        sch.dma("gpsimd", eb_all[:, 0:11 * 384].rearrange("p (h j) -> p h j", h=11),
                AP(expdram, 127, [[640, 128], [81920, 11], [1, 384]]),
                waits=[("ebD", 64), ("pe", sh_done)], inc="ebA")
        ST["gs_cons"] = list(gs_cons2)

    def expert_work():
        area = exp_area
        h2cons = [("ebD", 64)]
        gs_cons2 = list(ST.get("gs_cons", []))
        wv_g = list(WV_G0)
        wv_u = list(WV_U0)
        wv_d = list(WV_D0)
        wsem_g = ["wg0", "wg1"]
        wsem_u = ["wu0", "wu1"]
        wsem_d = ["wd0", "wd1"]
        egpat = [[DEXP, 128], [128 * DEXP, NCH], [1, DEXP]]
        edpat = [[D, 128], [128 * D, 4], [1, D]]
        for grp in range(4):
            if grp == 0:
                continue_loads = False
            else:
                continue_loads = True
            for si in range(2):
                if not continue_loads:
                    break
                ex = 2 * grp + si
                gv = AR_W[:, si * 6144:si * 6144 + 2048].bitcast(BF16)
                uv = AR_W[:, si * 6144 + 2048:si * 6144 + 4096].bitcast(BF16)
                dv = AR_W[:, si * 6144 + 4096:si * 6144 + 6144].bitcast(BF16)
                wq = [("pe", area[f"g{si}"])]
                wv_g[si] = sch.dma("gpsimd", gv.rearrange("p (c n) -> p c n", c=NCH),
                                   AP(w_eg, ex * D * DEXP, egpat), waits=wq, inc=wsem_g[si])
                wq = [("pe", area[f"u{si}"])]
                wv_u[si] = sch.dma("gpsimd", uv.rearrange("p (c n) -> p c n", c=NCH),
                                   AP(w_eu, ex * D * DEXP, egpat), waits=wq, inc=wsem_u[si])
                wq = [("pe", area["d"])]
                wv_d[si] = sch.dma("gpsimd", dv.rearrange("p (c n) -> p c n", c=4),
                                   AP(w_ed, ex * D * DEXP, edpat), waits=wq, inc=wsem_d[si])
            for t4 in range(4):
                for si in range(2):
                    kblk = (grp * 4 + t4) * 2 + si
                    scb = kblk % 2
                    cbv = cbs[scb]
                    gv = AR_W[:, si * 6144:si * 6144 + 2048].bitcast(BF16)
                    uv = AR_W[:, si * 6144 + 2048:si * 6144 + 4096].bitcast(BF16)
                    for dk in range(4):
                        w0 = pe_w("B6") + [(wsem_g[si], wv_g[si]), ("dve", ln_dve)]
                        for hc in range(NCH):
                            sch.ev("tensor", lambda e, t4=t4, dk=dk, hc=hc, gv=gv: e.matmul(
                                pB[6][:], gv[:, hc * 512 + dk * 128: hc * 512 + (dk + 1) * 128],
                                xn[:, hc * L + t4 * 512: hc * L + (t4 + 1) * 512],
                                start=(hc == 0), stop=(hc == NCH - 1)),
                                waits=(w0 if hc == 0 else []), inc="pe")
                        p_g = sch.cnt["pe"]
                        area[f"g{si}"] = p_g
                        w1 = pe_w("B7") + [(wsem_u[si], wv_u[si])]
                        for hc in range(NCH):
                            sch.ev("tensor", lambda e, t4=t4, dk=dk, hc=hc, uv=uv: e.matmul(
                                pB[7][:], uv[:, hc * 512 + dk * 128: hc * 512 + (dk + 1) * 128],
                                xn[:, hc * L + t4 * 512: hc * L + (t4 + 1) * 512],
                                start=(hc == 0), stop=(hc == NCH - 1)),
                                waits=(w1 if hc == 0 else []), inc="pe")
                        p_u = sch.cnt["pe"]
                        area[f"u{si}"] = p_u
                        sch.ev("scalar", lambda e: e.activation(gs[:], pB[6][:], AF.Tanh,
                                                                scale=hn_sb[:, 0:1]),
                               waits=[("pe", p_g), consts] + gs_cons2, inc="act")
                        hv = h2v[:, si * 2048 + dk * 512: si * 2048 + (dk + 1) * 512]
                        sch.ev("scalar", lambda e, hv=hv: e.activation(
                            hv, pB[6][:], AF.Copy, scale=hn_sb[:, 0:1]),
                            waits=h2cons, inc="act")
                        h2cons = []
                        consumed("B6", "act")
                        a_g = sch.cnt["act"]
                        dd = sch.ev("vector", lambda e, hv=hv: e.scalar_tensor_tensor(
                            hv, gs[:], 1.0, hv, ALU.add, ALU.mult),
                            waits=[("act", a_g)], inc="dve")
                        gs_cons2 = [("dve", sch.cnt["dve"])]
                        dd = sch.ev("vector", lambda e, hv=hv: e.tensor_mul(hv, hv, pB[7][:]),
                                    waits=[("pe", p_u), ("dve", dd)], inc="dve")
                        consumed("B7", "dve")
                        sch.ev("vector", lambda e, hv=hv, cbv=cbv: e.tensor_mul(
                            hv, hv, cbv[:]),
                            waits=[(f"cb{scb}", cb_v[scb]), ("dve", dd)], inc="dve")
                        yield 16
                    cb_cons[scb] = [("dve", sch.cnt["dve"])]
                    emit_cb(kblk + 2)
                d_h = sch.cnt["dve"]
                for hc in range(NCH):
                    pdi = 6 + hc % 2
                    reg = f"B{pdi}"
                    w0 = pe_w(reg) + [("dve", d_h),
                                      (wsem_d[0], wv_d[0]), (wsem_d[1], wv_d[1])]
                    first = True
                    for si in range(2):
                        dvv = AR_W[:, si * 6144 + 4096:si * 6144 + 6144].bitcast(BF16)
                        for dk in range(4):
                            sch.ev("tensor",
                                   lambda e, t4=t4, hc=hc, dk=dk, si=si, pdi=pdi, dvv=dvv:
                                   e.matmul(pB[pdi][:],
                                            dvv[:, dk * 1024 + hc * 128: dk * 1024 + (hc + 1) * 128],
                                            h2v[:, si * 2048 + dk * 512: si * 2048 + (dk + 1) * 512],
                                            start=(si == 0 and dk == 0),
                                            stop=(si == 1 and dk == 3)),
                                   waits=(w0 if first else []), inc="pe")
                            first = False
                    p_d = sch.cnt["pe"]
                    area["d"] = p_d
                    roff = hc * L + t4 * 512
                    ST["lrd"] = sch.ev("vector", lambda e, roff=roff, pdi=pdi: e.tensor_add(
                        resid[:, roff:roff + 512], resid[:, roff:roff + 512], pB[pdi][:]),
                        waits=[("pe", p_d), ("dve", ST["lrd"])], inc="dve")
                    consumed(reg, "dve")
                    yield 8
                h2cons = [("pe", sch.cnt["pe"])]

    def all_ffn():
        yield from shared_work()
        yield from expert_work()

    ffn_gen = all_ffn()
    ffn_alive = [True]
    SLOT = [0]
    FFN_MM = [0]
    TOT_MM = 768 + 3072

    def ffn_step():
        if ffn_alive[0]:
            try:
                FFN_MM[0] += next(ffn_gen)
            except StopIteration:
                ffn_alive[0] = False

    for g in range(8):
        ln_stats(g)
        if g >= 1:
            ln_finish(g - 1)
            gg = g - 1
            if gg >= 1:
                cap = min(64 * gg, 192 * ((gg + 1) // 2))
                while FFN_MM[0] < cap and ffn_alive[0]:
                    ffn_step()
        if g == 2:
            emit_p1_chain()
    ln_finish(7)
    ln_pe = sch.cnt["pe"]
    ln_act = sch.cnt["act"]
    ln_dve = sch.cnt["dve"]

    tap("xn", xn[:], [("dve", ln_dve)])
    tap("resid0", resid[:], [("act", ln_act)])
    if phases <= 0:
        return _finish(nc, sch)

    sch.ev("vector", lambda e: e.memset(Vaug[:, 64:2080:65], 1.0),
           waits=[("pe", ln_pe), ("act", ln_act)], inc="dve")

    PAIR_ORDER = [7, 6, 0, 1, 2, 3, 4, 5]
    wpair_load(PAIR_ORDER[0], 0, [])
    wpair_load(PAIR_ORDER[0], 1, [])
    eb_bc = h2t4[:, 0:2048]
    for q4e in range(4):
        sch.dma("gpsimd", eb_bc[:],
                AP(tdram, q4e * 4 * 512, [[0, 128], [1, 2048]]),
                waits=[("ebB", 16), ("ebD", 16 * q4e)], inc="ebC")
        sch.dma("gpsimd", AP(expdram, q4e * 4 * 81920, [[641, 128], [81920, 4], [1, 512]]),
                eb_bc.rearrange("p (h m) -> p h m", h=4),
                waits=[("ebC", 16 * (q4e + 1))], inc="ebD")
    sch.dma("gpsimd", eb_all[:, 11 * 384:16 * 384].rearrange("p (h j) -> p h j", h=5),
            AP(expdram, 127 + 11 * 81920, [[640, 128], [81920, 5], [1, 384]]),
            waits=[("ebD", 64)], inc="ebA")
    eb_readyA = ("ebA", sch.cnt["ebA"])
    eb_readyB = ("ebA", sch.cnt["ebA"] + 16)
    for ji, j in enumerate(PAIR_ORDER):
        pbanks = [2, 3, 6, 7] if ji == 0 else [0, 1, 2, 3]
        wpair_prep(j, [("pe", last_scores_pe)] if ji > 0 else [], pbanks)
        d_wp = sch.cnt["dve"]
        for wi, (which, colbase, scale) in enumerate(
                (("q", 2048, 0.125), ("k", 0, None), ("v", 1024, None))):
            dstt = {"q": Qp, "k": Kp, "v": Vfm}[which]
            for t4 in range(4):
                bi = pbanks[t4 % len(pbanks)]
                reg = f"B{bi}"
                pdst = pB[bi]
                w0 = pe_w(reg) + [("dve", d_wp), ("dve", ln_dve)]
                if which == "q":
                    w0 += [("pe", last_scores_pe)] if ji > 0 else []
                for dc in range(NCH):
                    sch.ev("tensor", lambda e, t4=t4, dc=dc, wi=wi, pdst=pdst: e.matmul(
                        pdst[:], w_pair[:, (wi * 8 + dc) * 128: (wi * 8 + dc + 1) * 128],
                        xn[:, dc * L + t4 * 512:dc * L + (t4 + 1) * 512],
                        start=(dc == 0), stop=(dc == NCH - 1)),
                        waits=(w0 if dc == 0 else []), inc="pe")
                pmm = sch.cnt["pe"]
                ew = [("pe", pmm)]
                if ji > 0:
                    ew.append(("pe", last_scores_pe))
                if scale is not None:
                    scap = hn_sb[:, 2:3]
                    sch.ev("scalar", lambda e, t4=t4, pdst=pdst, dstt=dstt, scap=scap:
                           e.activation(dstt[:, t4 * 512:(t4 + 1) * 512], pdst[:],
                                        AF.Copy, scale=scap[:, 0:1]),
                           waits=ew + [consts], inc="act")
                    consumed(reg, "act")
                else:
                    sch.ev("vector", lambda e, t4=t4, pdst=pdst, dstt=dstt:
                           e.tensor_copy(dstt[:, t4 * 512:(t4 + 1) * 512], pdst[:]),
                           waits=ew, inc="dve")
                    consumed(reg, "dve")
        a_pv = sch.cnt["act"]
        d_pv = sch.cnt["dve"]
        for tc in range(NT):
            bi = pbanks[tc % len(pbanks)]
            reg = f"B{bi}"
            pb = pBb[bi]
            sch.ev("tensor", lambda e, tc=tc, pb=pb: e.transpose(
                pb[:, 0:128], Vfm[:, tc * 128:(tc + 1) * 128], id_bf[:]),
                waits=pe_w(reg) + [("act", a_pv), ("dve", d_pv),
                                   ("pe", last_vaug_pe)], inc="pe")
            p_tr = sch.cnt["pe"]
            dst = Vaug[:, tc * 130: (tc + 1) * 130].rearrange(
                "p (g c) -> p g c", g=2)[:, :, 0:64]
            src = pb[:, 0:128].rearrange("p (g c) -> p g c", g=2)
            sch.ev("vector", lambda e, dst=dst, src=src: e.tensor_copy(dst, src),
                   waits=[("pe", p_tr)], inc="dve")
            consumed(reg, "dve")
        proj_evac_dve = sch.cnt["dve"]
        proj_evac_act = sch.cnt["act"]
        if ji == 0:
            router_exp()
        if ji < 7:
            wpair_load(PAIR_ORDER[ji + 1], 0, [("pe", sch.cnt["pe"])])
            wpair_load(PAIR_ORDER[ji + 1], 1, [("pe", sch.cnt["pe"])])

        for q4 in range(4):
            p_sc = {}
            a_exp_t = {}

            def tail_exp(kc):
                ping = kc % 2
                for hh in range(2):
                    sb = 2 * hh + ping
                    ab = 2 * hh + ping
                    sch.ev("scalar", lambda e, sb=sb, ab=ab: e.activation(
                        attb[ab][:], pB[sb][:], AF.Exp),
                        waits=[("pe", p_sc[(kc, hh)])] + attb_cons[ab], inc="act")
                    consumed(f"B{sb}", "act")
                    a_exp_t[(kc, hh)] = sch.cnt["act"]

            av_w = {}

            def tail_eb(kc):
                ping = kc % 2
                for hh in range(2):
                    ab = 2 * hh + ping
                    a_exp = a_exp_t[(kc, hh)]
                    h = 2 * j + hh
                    qt0 = q4 * 4
                    bt = [qt for qt in range(max(qt0, kc - 1), min(qt0 + 4, kc + 2))]
                    w_av = [("act", a_exp)]
                    d_eb = None
                    if bt:
                        c0 = (bt[0] - qt0) * 128
                        c1 = (bt[-1] - qt0) * 128 + 128
                        e0 = (bt[0] - kc + 1) * 128
                        e1 = (bt[-1] - kc + 1) * 128 + 128
                        ebr = eb_readyA if h >= 11 else eb_readyB
                        sch.ev("vector", lambda e, ab=ab, h=h, c0=c0, c1=c1, e0=e0, e1=e1:
                               e.tensor_mul(attb[ab][:, c0:c1], attb[ab][:, c0:c1],
                                            eb_all[:, h * 384 + e0:h * 384 + e1]),
                               waits=[("act", a_exp), ebr], inc="dve")
                        d_eb = sch.cnt["dve"]
                        w_av = [("dve", d_eb)]
                    av_w[(kc, hh)] = (w_av, d_eb)

            def tail(kc):
                ping = kc % 2
                for hh in range(2):
                    ab = 2 * hh + ping
                    w_av, d_eb = av_w[(kc, hh)]
                    reg = "B4" if hh == 0 else "B5"
                    w2 = list(w_av)
                    if kc == 0:
                        w2 += avdone[reg]
                    sch.ev("tensor", lambda e, hh=hh, kc=kc, ab=ab: e.matmul(
                        pB[4 + hh][0:65, :],
                        Vaug[:, kc * 130 + hh * 65: kc * 130 + (hh + 1) * 65],
                        attb[ab][:],
                        start=(kc == 0), stop=(kc == NT - 1)),
                        waits=w2, inc="pe")
                    attb_cons[ab] = [("pe", sch.cnt["pe"])]
                    if d_eb is not None:
                        attb_cons[ab].append(("dve", d_eb))

            for kc in range(NT):
                ping = kc % 2
                w_pair2 = (pe_w(f"B{0 + ping}") + pe_w(f"B{2 + ping}")
                           + [("dve", proj_evac_dve), ("act", proj_evac_act)])
                for hh in range(2):
                    sb = 2 * hh + ping
                    base = 64 * hh
                    sch.ev("tensor", lambda e, kc=kc, q4=q4, sb=sb, base=base: e.matmul(
                        pB[sb][:], Kp[base:base + 64, kc * 128:(kc + 1) * 128],
                        Qp[base:base + 64, q4 * 512:(q4 + 1) * 512],
                        start=True, stop=True, tile_position=(base, 0)),
                        waits=(w_pair2 if hh == 0 else []), inc="pe")
                    p_sc[(kc, hh)] = sch.cnt["pe"]
                if kc >= 1:
                    tail_exp(kc - 1)
                    tail_eb(kc - 1)
                SLOT[0] += 1
                tgt = (TOT_MM * SLOT[0]) // 512
                while FFN_MM[0] < tgt and ffn_alive[0]:
                    ffn_step()
                if kc >= 1:
                    tail(kc - 1)
                if ji == 0 and q4 == 0:
                    router_chain_step(kc)
                    coefT_step()
            tail_exp(NT - 1)
            tail_eb(NT - 1)
            tail(NT - 1)
            last_vaug_pe = sch.cnt["pe"]
            p_av = sch.cnt["pe"]
            newpend = []
            for hh in range(2):
                reg = "B4" if hh == 0 else "B5"
                slot = 2 * hh + (q4 % 2)
                sch.ev("vector", lambda e, hh=hh, slot=slot: e.tensor_copy(
                    av_ep[slot][:], pB[4 + hh][0:65, :]),
                    waits=[("pe", p_av)] + av_ep_cons[slot], inc="dve")
                avdone[reg] = [("dve", sch.cnt["dve"])]
                newpend.append({"hh": hh, "slot": slot, "q4": q4, "h": 2 * j + hh,
                                "evac": ("dve", sch.cnt["dve"])})
            epi_stage2()
            pend = newpend
            epi_stage1()
        last_scores_pe = sch.cnt["pe"]

    epi_stage2()
    pend = []
    while ffn_alive[0]:
        ffn_step()

    attn_dve = sch.cnt["dve"]
    attn_act = sch.cnt["act"]
    tap("resid_attn", resid[:], [("dve", attn_dve)])
    if phases <= 4:
        return _finish(nc, sch)

    tap("resid_sh", resid[:], [("dve", sch.cnt["dve"])])

    tap("resid_moe", resid[:], [("dve", sch.cnt["dve"])])
    if phases <= 7:
        return _finish(nc, sch)

    outst = [AR_M[:, 0:2048], AR_M[:, 2048:4096]]
    p7_dve = sch.cnt["dve"]
    out_cons = [[("pe", sch.cnt["pe"]), ("dve", p7_dve)],
                [("pe", sch.cnt["pe"]), ("dve", p7_dve)]]
    for g in range(8):
        sl = g % 2
        for ti in range(2):
            tc = 2 * g + ti
            for half in range(2):
                reg = ["B6", "B7"][(2 * ti + half) % 2]
                pb = pB[6 + (2 * ti + half) % 2][:]
                dcs = [half * 4 + i for i in range(4)]
                ws = pe_w(reg) + [("dve", p7_dve)]
                for i, dc in enumerate(dcs):
                    sch.ev("tensor", lambda e, i=i, dc=dc, tc=tc, pb=pb: e.transpose(
                        pb[:, i * 128:(i + 1) * 128],
                        resid[:, dc * L + tc * 128:dc * L + (tc + 1) * 128], id_f32[:]),
                        waits=(ws if i == 0 else []), inc="pe")
                p_t = sch.cnt["pe"]
                sch.ev("vector", lambda e, sl=sl, ti=ti, half=half, pb=pb: e.tensor_copy(
                    outst[sl][:, ti * 1024 + half * 512: ti * 1024 + (half + 1) * 512], pb[:]),
                    waits=[("pe", p_t)] + (out_cons[sl] if ti == 0 and half == 0 else []),
                    inc="dve")
                consumed(reg, "dve")
        d_o = sch.cnt["dve"]
        sch.dma("sync", AP(y_out, g * 256 * D, [[D, 128], [128 * D, 2], [1, D]]),
                outst[sl].rearrange("p (t c) -> p t c", t=2),
                waits=[("dve", d_o)], inc="outq")
        out_cons[sl] = [("outq", sch.cnt["outq"])]
    sch.ev("sync", lambda e: e.nop(), waits=[("outq", sch.cnt["outq"])])

    return _finish(nc, sch)


def _finish(nc, sch):
    with nc.allow_low_precision(reason="bf16 score writeout by design"):
        with nc.Block() as block:
            sch.emit(block)
    return nc


def _shard_inputs(inputs):
    x = np.ascontiguousarray(inputs["L0_attn_x"], dtype=np.float32)
    common = {
        "w_in": np.ascontiguousarray(inputs["L0_attn_w_in"], dtype=np.float32),
        "k_off": np.ascontiguousarray(inputs["L0_attn_kernel_offsets"], dtype=np.float32),
        "k_amp": np.ascontiguousarray(inputs["L0_attn_kernel_amplitudes"], dtype=np.float32),
        "k_sh": np.ascontiguousarray(inputs["L0_attn_kernel_sharpness"], dtype=np.float32),
        "w_rtr": np.ascontiguousarray(inputs["L0_ffn_router"], dtype=np.float32),
        "w_eg": np.ascontiguousarray(inputs["L0_ffn_experts_gate"], dtype=np.float32),
        "w_eu": np.ascontiguousarray(inputs["L0_ffn_experts_up"], dtype=np.float32),
        "w_ed": np.ascontiguousarray(inputs["L0_ffn_experts_down"], dtype=np.float32),
        "w_sg": np.ascontiguousarray(inputs["L0_ffn_shared_gate"], dtype=np.float32),
        "w_su": np.ascontiguousarray(inputs["L0_ffn_shared_up"], dtype=np.float32),
        "w_sd": np.ascontiguousarray(inputs["L0_ffn_shared_down"], dtype=np.float32),
    }
    return [dict(common, x=np.ascontiguousarray(x[b])) for b in range(B)]


def kernel(**inputs):
    nc = build_program()
    in_maps = _shard_inputs(inputs)
    res = run_bass_kernel_spmd(nc, in_maps, core_ids=list(range(B)))
    return np.stack([r["y"] for r in res.results], axis=0).astype(np.float32)


if __name__ == "__main__":
    nc = build_program(phases=0)
    print("build ok; instructions:", nc._state().n_instructions()
          if hasattr(nc, "_state") else "?")
